# version 19
# baseline (speedup 1.0000x reference)
import numpy as np
import concourse.bass as bass
import concourse.bacc as bacc_mod
import concourse.mybir as mybir
from concourse.bass_utils import run_bass_kernel_spmd
from contextlib import ExitStack

F32 = mybir.dt.float32
BF16 = mybir.dt.bfloat16
I32 = mybir.dt.int32
AF = mybir.ActivationFunctionType
ALU = mybir.AluOpType
AX = mybir.AxisListType
ts = bass.ts
ds = bass.ds

P = 128
D = 1024
I_FFN = 2816
NH = 16
HD = 64
VW = NH * (HD + 1)
C1 = float(3 * 2 ** 21)
C2 = float(3 * 2 ** 22)
C3 = float(3 * 2 ** 23)
NEG = -1.0e9


from contextlib import contextmanager


@contextmanager
def Forw(nc, start, end):
    from concourse.bass import sinput_min_value, sinput_max_value
    engines = mybir.ALL_ENGINES
    loop_id = nc.next_id()
    loop_start = f"forw_{loop_id}_loop"
    loop_end = f"forw_{loop_id}_end"
    registers = nc.alloc_registers(f"forw_{loop_id}_i", engines=engines)
    nc.regs_mov(registers, start)
    nc.br_lt(registers, end, on_true=loop_start, on_false=loop_end,
             engines=engines)
    min_val = sinput_min_value(start)
    max_val = sinput_max_value(end)
    if max_val is not None:
        max_val -= 1
    with nc.body(loop_start, valid_engines=engines):
        yield nc.snap(registers, min_val=min_val, max_val=max_val)
        nc.regs_alu(registers, registers, 1, op=mybir.AluOpType.add)
        nc.br_lt(registers, end, on_true=loop_start, on_false=loop_end,
                 engines=engines)
    nc.switch_bb(loop_end)


class _Sem:
    def __init__(self, handle):
        self.h = handle
        self.pre = 0
        self.per = None
        self.cur = 0


class _InstStub:
    def __init__(self, ctx):
        self.ctx = ctx

    def then_inc(self, semname, n=1):
        self.ctx._count(semname, n)
        return self


class _InstWrap:
    def __init__(self, ctx, inst):
        self.ctx = ctx
        self.inst = inst

    def then_inc(self, semname, n=1):
        s = self.ctx.sems[semname]
        self.inst.then_inc(s.h, n)
        self.ctx._count(semname, n)
        return self


class _EngProxy:
    def __init__(self, ctx, eng):
        self._ctx = ctx
        self._eng = eng

    def __getattr__(self, name):
        ctx = self._ctx
        eng = self._eng

        def call(*a, **k):
            if ctx.counting:
                return _InstStub(ctx)
            return _InstWrap(ctx, getattr(eng, name)(*a, **k))
        return call


class Ctx:
    def __init__(self, nc, sems, counting, inner_reps=0):
        self.nc = nc
        self.sems = sems
        self.counting = counting
        self.inner_reps = inner_reps
        self.inner_pers_list = []
        self.off = {}
        self._per = {}
        self.marks_out = {}
        self.premarks = {}
        self._inner_idx = 0
        self.TE = _EngProxy(self, nc.tensor)
        self.V = _EngProxy(self, nc.vector)
        self.S = _EngProxy(self, nc.scalar)
        self.G = _EngProxy(self, nc.gpsimd)
        self.SY = _EngProxy(self, nc.sync)

    def _count(self, semname, n):
        self.sems[semname].cur += n

    def wait(self, engproxy, semname, back=0):
        if self.counting:
            return
        s = self.sems[semname]
        val = s.pre + s.cur - back
        off = self.off.get(semname, 0)
        expr = off + val if not isinstance(off, int) or off else val
        if isinstance(expr, int):
            assert expr >= 0, f"negative wait on {semname}"
        engproxy._eng.wait_ge(s.h, expr)

    def mark(self, semname):
        return (semname, self.sems[semname].cur)

    def set_mark(self, key, semname):
        self.marks_out[key] = (semname, self.sems[semname].cur)

    def get_mark(self, key):
        if key in self.marks_out:
            return self.marks_out[key]
        return self.premarks.get(key)

    def wait_mark(self, engproxy, mk, lag=0):
        if self.counting or mk is None:
            return
        semname, rel = mk
        s = self.sems[semname]
        val = s.pre + rel - lag * self._per.get(semname, 0)
        off = self.off.get(semname, 0)
        expr = off + val if not isinstance(off, int) or off else val
        if isinstance(expr, int):
            assert expr >= 0, f"negative mark-wait on {semname}"
        engproxy._eng.wait_ge(s.h, expr)

    def inner(self, trip, fn):
        if self.counting:
            snap = {n: s.cur for n, s in self.sems.items()}
            for _ in range(self.inner_reps):
                fn(self, 0)
            pers = {n: s.cur - snap[n] for n, s in self.sems.items()}
            self.inner_pers_list.append(pers)
            return
        if isinstance(trip, int) and trip == 0:
            if self.inner_pers_list:
                self._inner_idx += 1
            return
        pers = self.inner_pers_list[self._inner_idx]
        self._inner_idx += 1
        if isinstance(trip, int):
            for kc in range(trip):
                sub = Ctx(self.nc, self.sems, counting=False)
                sub.off = {}
                curs = {}
                for n, s in self.sems.items():
                    o = self.off.get(n, 0)
                    extra = kc * pers[n]
                    sub.off[n] = (o + extra) if not (isinstance(o, int) and o == 0 and extra == 0) else 0
                    curs[n] = s.cur
                fn(sub, kc)
                for n, s in self.sems.items():
                    s.cur = curs[n]
            for n, s in self.sems.items():
                self.off[n] = self.off.get(n, 0) + trip * pers[n]
            return
        with Forw(self.nc, 0, trip) as kc:
            sub = Ctx(self.nc, self.sems, counting=False)
            curs = {}
            for n, s in self.sems.items():
                o = self.off.get(n, 0)
                sub.off[n] = o + kc * pers[n] if pers[n] else o
                curs[n] = s.cur
            fn(sub, kc)
            for n, s in self.sems.items():
                s.cur = curs[n]
        for n, s in self.sems.items():
            if pers[n]:
                self.off[n] = self.off.get(n, 0) + trip * pers[n]


class SemSet:
    def __init__(self, nc, stack, names):
        self.nc = nc
        self.sems = {}
        for n in names:
            self.sems[n] = _Sem(stack.enter_context(nc.semaphore(n)))

    def _count(self, body, inner_reps):
        cnt = Ctx(self.nc, self.sems, counting=True, inner_reps=inner_reps)
        saved = {n: s.cur for n, s in self.sems.items()}
        for s in self.sems.values():
            s.cur = 0
        body(cnt, 0)
        pers = {n: s.cur for n, s in self.sems.items()}
        ipl = cnt.inner_pers_list
        for n, s in self.sems.items():
            s.cur = saved[n]
        return pers, ipl, cnt.marks_out

    def run_loop(self, nc, g0, gend, body, tri=False, inner_cums=None):
        if gend <= g0:
            return
        persA, iplA, marksA = self._count(body, 0)
        if tri or inner_cums is not None:
            persB, iplB, marksB = self._count(body, 1)
            inner_pers = iplB
            base_pers = persA
        else:
            inner_pers = iplA
            base_pers = persA
        with nc.Fori(g0, gend) as gv:
            ctx = Ctx(nc, self.sems, counting=False)
            ctx.inner_pers_list = inner_pers
            ctx._per = dict(base_pers)
            ctx.premarks = marksA
            grel = gv - g0 if g0 else gv
            if tri:
                trisum = (gv * (gv - 1)) // 2 - (g0 * (g0 - 1)) // 2
            for n, s in self.sems.items():
                o = 0
                if base_pers[n]:
                    o = grel * base_pers[n]
                if tri:
                    stot = sum(pc[n] for pc in inner_pers)
                    if stot:
                        o = o + trisum * stot
                elif inner_cums is not None:
                    for (cum_fn, _tot), pc in zip(inner_cums, inner_pers):
                        if pc[n]:
                            o = o + cum_fn(gv) * pc[n]
                ctx.off[n] = o
            body(ctx, gv)
        ninter = gend - g0
        tot_tri = sum(range(g0, gend))
        for n, s in self.sems.items():
            s.pre += ninter * base_pers[n]
            if tri:
                stot = sum(pc[n] for pc in inner_pers)
                s.pre += tot_tri * stot
            elif inner_cums is not None:
                for (_cf, tot), pc in zip(inner_cums, inner_pers):
                    s.pre += tot * pc[n]
            s.cur = 0

    def straight(self, nc):
        return Ctx(nc, self.sems, counting=False)

    def flush_straight(self):
        for s in self.sems.values():
            s.pre += s.cur
            s.cur = 0


def build_nc(T=4096, do_ffn1=True, do_attn=True, do_ffn2=True):
    NT = T // P
    NG = T // 512
    nc = bacc_mod.Bacc("TRN2")

    x_in = nc.dram_tensor("x", [T, D], F32, kind="ExternalInput").ap()
    wg_in = nc.dram_tensor("wg", [I_FFN, D], F32, kind="ExternalInput").ap()
    wu_in = nc.dram_tensor("wu", [I_FFN, D], F32, kind="ExternalInput").ap()
    wd_in = nc.dram_tensor("wd", [D, I_FFN], F32, kind="ExternalInput").ap()
    wqkvT_in = nc.dram_tensor("wqkvT", [D, 3 * D], F32, kind="ExternalInput").ap()
    wpT_in = nc.dram_tensor("wpT", [D, D], F32, kind="ExternalInput").ap()
    out = nc.dram_tensor("out", [T, D], F32, kind="ExternalOutput").ap()

    qwg = nc.dram_tensor("qwg", [D, I_FFN], BF16).ap()
    qwu = nc.dram_tensor("qwu", [D, I_FFN], BF16).ap()
    qwd = nc.dram_tensor("qwd", [I_FFN, D], BF16).ap()
    Qs = nc.dram_tensor("Qs", [D, T], BF16).ap()
    Ks = nc.dram_tensor("Ks", [D, T], BF16).ap()
    Vs = nc.dram_tensor("Vs", [T, VW], BF16).ap()
    wqkvb = nc.dram_tensor("wqkvb", [D, 3 * D], BF16).ap()
    escr = nc.dram_tensor("escr", [4 * 512], F32).ap()
    abscr = nc.dram_tensor("abscr", [2, 4 * 512], BF16).ap()
    vascr = nc.dram_tensor("vascr", [VW], F32).ap()

    with ExitStack() as stack:
        def psum(name, shape, dt=F32):
            return stack.enter_context(nc.psum_tensor(name, shape, dt))

        def sbuf(name, shape, dt=F32):
            return stack.enter_context(nc.sbuf_tensor(name, shape, dt))

        stream = sbuf("stream", [P, NT * D])
        ident_b = sbuf("ident_b", [P, P], BF16)
        ident_f = sbuf("ident_f", [P, P])
        trimask = sbuf("trimask", [P, P])
        trimaskT = sbuf("trimaskT", [P, P])
        epscol = sbuf("epscol", [P, 1])
        ones_row = sbuf("ones_row", [1, 512], BF16)
        zero_row = sbuf("zero_row", [1, P], BF16)
        iot = stack.enter_context(nc.sbuf_tensor("iot", [P, P], I32))

        psALL = psum("psALL", [P, 8 * 512])

        def bank(i, w=512):
            return psALL[:, i * 512:i * 512 + w]

        def bankb(i):
            return psALL[:].bitcast(BF16)[:, i * 1024:(i + 1) * 1024]

        SS = SemSet(nc, stack, ["dx", "dw", "dw2", "dqw", "dk", "dq", "dkv",
                                "pe", "vv", "ss", "gg", "st"])

        def arena_fence():
            cfz = SS.straight(nc)
            for eng in (cfz.V, cfz.S, cfz.G, cfz.TE, cfz.SY):
                for sm in ("dx", "dw", "dw2", "dqw", "dk", "dq", "dkv", "st"):
                    cfz.wait(eng, sm)
                for sm in ("vv", "ss", "gg", "pe"):
                    cfz.wait(eng, sm)
            SS.flush_straight()

        c0 = SS.straight(nc)
        c0.G.iota(out=iot[:], pattern=[[1, P]], base=0,
                  channel_multiplier=-1).then_inc("gg", 1)
        c0.G.memset(epscol[:], 1e-5).then_inc("gg", 1)
        c0.G.memset(ones_row[:], 1.0).then_inc("gg", 1)
        c0.G.memset(zero_row[:], 0.0).then_inc("gg", 1)
        c0.G.dma_start(out=wqkvb, in_=wqkvT_in).then_inc("dw2", 16)
        c0.SY.dma_start(out=stream[:, 0:NT * D].rearrange("p (t d) -> p t d", d=D),
                        in_=x_in.rearrange("(t p) d -> p t d", p=P)
                        ).then_inc("dx", 16)
        c0.wait(c0.V, "gg")
        c0.V.tensor_scalar(out=ident_b[:], in0=iot[:], scalar1=0, scalar2=1.0,
                           op0=ALU.is_equal, op1=ALU.mult).then_inc("vv", 1)
        c0.V.tensor_scalar(out=ident_f[:], in0=iot[:], scalar1=0, scalar2=1.0,
                           op0=ALU.is_equal, op1=ALU.mult).then_inc("vv", 1)
        c0.V.tensor_scalar(out=trimask[:], in0=iot[:], scalar1=0, scalar2=NEG,
                           op0=ALU.is_gt, op1=ALU.mult).then_inc("vv", 1)
        c0.V.tensor_scalar(out=trimaskT[:], in0=iot[:], scalar1=0, scalar2=NEG,
                           op0=ALU.is_lt, op1=ALU.mult).then_inc("vv", 1)
        SS.flush_straight()

        def make_quant_bufs(ar, tA=None, pairs=False, alt=None):
            B = {}
            if tA is None:
                B["tA"] = [ar.sbuf_tensor(f"q_tA{i}", [P, 512], F32) for i in range(2)]
            else:
                B["tA"] = tA
            if pairs:
                B["tB"] = [ar.sbuf_tensor(f"q_tB{i}", [P, 512], F32) for i in range(2)]
                B["sgn"] = [ar.sbuf_tensor(f"q_sgn{i}", [P, 512], BF16) for i in range(2)]
            elif alt is not None:
                B["tB"] = [ar.sbuf_tensor("q_tB0", [P, 512], F32), alt[0]]
                B["sgn"] = [ar.sbuf_tensor("q_sgn0", [P, 512], BF16), alt[1]]
            else:
                B["tB"] = [ar.sbuf_tensor("q_tB0", [P, 512], F32)] * 2
                B["sgn"] = [ar.sbuf_tensor("q_sgn0", [P, 512], BF16)] * 2
            B["tC"] = ar.sbuf_tensor("q_tC", [P, 512], F32)
            B["Lb"] = ar.sbuf_tensor("q_Lb", [P, 512], BF16)
            B["gsb"] = ar.sbuf_tensor("q_gsb", [P, 512], BF16)
            B["g1b"] = B["gsb"]
            B["Mblk"] = ar.sbuf_tensor("q_Mblk", [P, 32], F32)
            B["s6"] = [ar.sbuf_tensor(f"q_s6{i}", [P, 32], F32) for i in range(2)]
            B["r6"] = B["Mblk"]
            return B

        def make_ln_bufs(ar, jnk=None):
            B = {}
            B["lncol"] = ar.sbuf_tensor("ln_col", [P, 8], F32)
            B["xlnt"] = ar.sbuf_tensor("ln_x", [P, D], F32)
            if jnk is None:
                B["jnk1"] = B["xlnt"][:, 0:512]
                B["jnk2"] = B["xlnt"][:, 512:1024]
            else:
                B["jnk1"], B["jnk2"] = jnk
            return B

        def equant_v1(ctx, B, src_ap, w, sl=0):
            nblk = w // 16
            src3 = src_ap.rearrange("p (g b) -> p g b", b=16)
            ctx.wait(ctx.V, "vv")
            ctx.V.tensor_reduce(out=B["Mblk"][:, 0:nblk], in_=src3, axis=AX.X,
                                op=ALU.max,
                                apply_absolute_value=True).then_inc("vv", 1)
            ctx.wait(ctx.V, "vv")
            ctx.V.tensor_scalar(out=B["s6"][sl][:, 0:nblk], in0=B["Mblk"][:, 0:nblk],
                                scalar1=6e-12, scalar2=1.0 / 6.0,
                                op0=ALU.max, op1=ALU.mult).then_inc("vv", 1)
            ctx.wait(ctx.V, "vv")
            ctx.V.reciprocal(out=B["r6"][:, 0:nblk],
                             in_=B["s6"][sl][:, 0:nblk]).then_inc("vv", 1)
            ctx.wait(ctx.V, "vv")
            r4 = B["r6"][:, 0:nblk].unsqueeze(2).unsqueeze(3).broadcast_to([P, nblk, 16, 1])
            ctx.V.tensor_tensor(
                out=B["tA"][sl][:, 0:w].rearrange("p (g b) -> p g b", b=16).unsqueeze(3),
                in0=src3.unsqueeze(3), in1=r4, op=ALU.mult).then_inc("vv", 1)

        def equant_s(ctx, B, w, sl=0, skip_wait=False, slb=None):
            slb = sl if slb is None else slb
            if not skip_wait:
                ctx.wait(ctx.S, "vv")
            ctx.S.activation(out=B["tB"][slb][:, 0:w], in_=B["tA"][sl][:, 0:w],
                             func=AF.Abs).then_inc("ss", 1)
            ctx.S.activation(out=B["sgn"][slb][:, 0:w], in_=B["tA"][sl][:, 0:w],
                             func=AF.Sign).then_inc("ss", 1)

        def equant_v2(ctx, B, dst_ap, w, sl=0, skip_wait=False, slb=None):
            nblk = w // 16
            slb = sl if slb is None else slb
            tB, tC = B["tB"][slb], B["tC"]
            Lb = B["Lb"]
            g1b, gsb = B["g1b"], B["gsb"]
            if not skip_wait:
                ctx.wait(ctx.V, "ss")
            ctx.V.tensor_scalar(out=tC[:, 0:w], in0=tB[:, 0:w], scalar1=4.0,
                                scalar2=C3, op0=ALU.max,
                                op1=ALU.add).then_inc("vv", 1)
            ctx.wait(ctx.V, "vv")
            ctx.V.tensor_scalar(out=gsb[:, 0:w], in0=tC[:, 0:w], scalar1=C3,
                                scalar2=6.0, op0=ALU.subtract,
                                op1=ALU.subtract).then_inc("vv", 1)
            ctx.wait(ctx.V, "vv")
            ctx.V.tensor_scalar(out=tC[:, 0:w], in0=tB[:, 0:w], scalar1=2.0,
                                scalar2=C1, op0=ALU.min,
                                op1=ALU.add).then_inc("vv", 1)
            ctx.wait(ctx.V, "vv")
            ctx.V.scalar_tensor_tensor(out=g1b[:, 0:w], in0=tC[:, 0:w],
                                       scalar=C1, in1=g1b[:, 0:w],
                                       op0=ALU.subtract,
                                       op1=ALU.add).then_inc("vv", 1)
            ctx.wait(ctx.V, "vv")
            ctx.V.tensor_scalar(out=tC[:, 0:w], in0=tB[:, 0:w], scalar1=2.0,
                                scalar2=4.0, op0=ALU.max,
                                op1=ALU.min).then_inc("vv", 1)
            ctx.wait(ctx.V, "vv")
            ctx.V.tensor_scalar(out=tC[:, 0:w], in0=tC[:, 0:w], scalar1=C2,
                                scalar2=None, op0=ALU.add).then_inc("vv", 1)
            ctx.wait(ctx.V, "vv")
            ctx.V.scalar_tensor_tensor(out=g1b[:, 0:w], in0=tC[:, 0:w],
                                       scalar=C2, in1=g1b[:, 0:w],
                                       op0=ALU.subtract,
                                       op1=ALU.add).then_inc("vv", 1)
            ctx.wait(ctx.V, "vv")
            ctx.V.tensor_tensor(out=gsb[:, 0:w], in0=gsb[:, 0:w],
                                in1=B["sgn"][slb][:, 0:w],
                                op=ALU.mult).then_inc("vv", 1)
            ctx.wait(ctx.V, "vv")
            s4 = B["s6"][sl][:, 0:nblk].unsqueeze(2).unsqueeze(3).broadcast_to([P, nblk, 16, 1])
            ctx.V.tensor_tensor(
                out=dst_ap.rearrange("p (g b) -> p g b", b=16).unsqueeze(3),
                in0=gsb[:, 0:w].rearrange("p (g b) -> p g b", b=16).unsqueeze(3),
                in1=s4, op=ALU.mult).then_inc("vv", 1)

        def emit_quant(ctx, B, src_ap, dst_ap, w, sl=0):
            equant_v1(ctx, B, src_ap, w, sl)
            equant_s(ctx, B, w, sl)
            equant_v2(ctx, B, dst_ap, w, sl)

        def emit_ln(ctx, B, xsl):
            lncol, xlnt = B["lncol"], B["xlnt"]
            jnk1, jnk2 = B["jnk1"], B["jnk2"]
            ctx.wait(ctx.S, "vv")
            ctx.wait(ctx.S, "gg")
            ctx.S.activation(out=jnk1, in_=xsl[:, 0:512], func=AF.Square,
                             accum_out=lncol[:, 0:1]).then_inc("ss", 1)
            ctx.S.activation(out=jnk2, in_=xsl[:, 512:1024], func=AF.Square,
                             accum_out=lncol[:, 7:8]).then_inc("ss", 1)
            ctx.V.tensor_reduce(out=lncol[:, 1:2], in_=xsl, axis=AX.X,
                                op=ALU.add).then_inc("vv", 1)
            ctx.wait(ctx.V, "ss")
            ctx.V.tensor_tensor(out=lncol[:, 0:1], in0=lncol[:, 0:1],
                                in1=lncol[:, 7:8], op=ALU.add).then_inc("vv", 1)
            ctx.wait(ctx.V, "vv")
            ctx.V.tensor_scalar(out=lncol[:, 2:3], in0=lncol[:, 1:2],
                                scalar1=1.0 / D, scalar2=None,
                                op0=ALU.mult).then_inc("vv", 1)
            ctx.wait(ctx.V, "vv")
            ctx.V.tensor_tensor(out=lncol[:, 3:4], in0=lncol[:, 2:3],
                                in1=lncol[:, 2:3], op=ALU.mult).then_inc("vv", 1)
            ctx.wait(ctx.V, "ss")
            ctx.wait(ctx.V, "vv")
            ctx.V.scalar_tensor_tensor(out=lncol[:, 4:5], in0=lncol[:, 0:1],
                                       scalar=1.0 / D, in1=lncol[:, 3:4],
                                       op0=ALU.mult,
                                       op1=ALU.subtract).then_inc("vv", 1)
            ctx.wait(ctx.S, "vv")
            ctx.S.activation(out=lncol[:, 5:6], in_=lncol[:, 4:5],
                             func=AF.Sqrt, bias=epscol[:, 0:1],
                             scale=1.0).then_inc("ss", 1)
            ctx.wait(ctx.V, "ss")
            ctx.V.reciprocal(out=lncol[:, 6:7],
                             in_=lncol[:, 5:6]).then_inc("vv", 1)
            ctx.wait(ctx.V, "vv")
            ctx.V.tensor_scalar(out=xlnt[:], in0=xsl, scalar1=lncol[:, 2:3],
                                scalar2=lncol[:, 6:7], op0=ALU.subtract,
                                op1=ALU.mult).then_inc("vv", 1)

        wq_ar = nc.auto_arena()
        QB_wq = make_quant_bufs(wq_ar, pairs=True)
        wqin = [wq_ar.sbuf_tensor(f"wqin{i}", [P, I_FFN], F32) for i in range(2)]
        wqout = [wq_ar.sbuf_tensor(f"wqout{i}", [P, I_FFN], BF16) for i in range(2)]
        wqTev = [wq_ar.sbuf_tensor(f"wqTev{i}", [P, 22 * P], BF16) for i in range(2)]
        wq_ar.allocate()

        def make_wq_body(src_dram, dst_dram, width, ntiles):
            ntile_t = width // P
            ncw = (width + 511) // 512

            def emit_tp_store(ctx, slot, r):
                ctx.wait(ctx.TE, "vv")
                ctx.wait(ctx.S, "dqw")
                for j in range(ntile_t):
                    gj = j % 2
                    if gj == 0:
                        ctx.wait(ctx.TE, "ss")
                    w_ = ctx.TE.transpose(
                        out=bankb(6)[:, gj * P:(gj + 1) * P],
                        in_=wqout[slot][:, ts(j, P)], identity=ident_b[:])
                    if gj == 1 or j == ntile_t - 1:
                        w_.then_inc("pe", 1)
                        ctx.wait(ctx.S, "pe")
                        ctx.S.activation(
                            out=wqTev[slot][:, (j - gj) * P:(j + 1) * P],
                            in_=bankb(6)[:, 0:(gj + 1) * P],
                            func=AF.Copy, bias=0.0,
                            scale=1.0).then_inc("ss", 1)
                ctx.wait(ctx.SY, "ss")
                ctx.SY.dma_start(
                    out=dst_dram.rearrange("(dc p) c -> p dc c", p=P)
                    [:, :, ds(r * P, P)],
                    in_=wqTev[slot][:, 0:ntile_t * P].rearrange(
                        "p (dc c) -> p dc c", c=P)).then_inc("dqw", 16)

            def body(ctx, r2):
                items = [(s, c) for s in range(2) for c in range(ncw)]
                ctx.wait(ctx.V, "dw")
                ctx.wait(ctx.V, "pe")
                for idx, (s, c) in enumerate(items):
                    w512 = min(512, width - c * 512)
                    sl = idx % 2
                    equant_v1(ctx, QB_wq, wqin[s][:, c * 512:c * 512 + w512],
                              w512, sl=sl)
                    equant_s(ctx, QB_wq, w512, sl=sl)
                    if idx > 0:
                        sp, cp = items[idx - 1]
                        wp = min(512, width - cp * 512)
                        ctx.wait(ctx.V, "ss", back=2)
                        equant_v2(ctx, QB_wq,
                                  wqout[sp][:, cp * 512:cp * 512 + wp], wp,
                                  sl=(idx - 1) % 2, skip_wait=True)
                        if cp == ncw - 1:
                            emit_tp_store(ctx, sp, 2 * r2 + sp)
                    if c == ncw - 1:
                        ctx.wait(ctx.SY, "vv")
                        ctx.wait(ctx.SY, "dw")
                        ctx.SY.dma_start(
                            out=wqin[s][:, 0:width],
                            in_=src_dram.rearrange("(t p) d -> t p d", p=P)
                            [(2 * r2 + s + 2) % ntiles]).then_inc("dw", 16)
                sp, cp = items[-1]
                wp = min(512, width - cp * 512)
                ctx.wait(ctx.V, "ss")
                equant_v2(ctx, QB_wq, wqout[sp][:, cp * 512:cp * 512 + wp],
                          wp, sl=(len(items) - 1) % 2, skip_wait=True)
                emit_tp_store(ctx, sp, 2 * r2 + sp)
            return body

        def wq_prologue(src_dram, width):
            cpro = SS.straight(nc)
            for s in range(2):
                cpro.wait(cpro.SY, "dw")
                cpro.SY.dma_start(
                    out=wqin[s][:, 0:width],
                    in_=src_dram.rearrange("(t p) d -> t p d", p=P)[s]
                    ).then_inc("dw", 16)
            SS.flush_straight()

        wq_prologue(wg_in, D)
        SS.run_loop(nc, 0, I_FFN // P // 2, make_wq_body(wg_in, qwg, D, I_FFN // P))
        wq_prologue(wu_in, D)
        SS.run_loop(nc, 0, I_FFN // P // 2, make_wq_body(wu_in, qwu, D, I_FFN // P))
        wq_prologue(wd_in, I_FFN)
        SS.run_loop(nc, 0, D // P // 2, make_wq_body(wd_in, qwd, I_FFN, D // P))
        arena_fence()
        wq_ar.free()

        def emit_ffn_phase():
            ar = nc.auto_arena()
            LB = make_ln_bufs(ar, jnk=None)
            QB = make_quant_bufs(ar, alt=(
                LB["xlnt"][:, 0:512],
                LB["xlnt"][:].bitcast(BF16)[:, 1024:1536]))
            xqT = ar.sbuf_tensor("xqT", [P, 32 * P], BF16)
            wgb = [ar.sbuf_tensor(f"wgb{i}", [P, 8 * 512], BF16) for i in range(2)]
            wub = [ar.sbuf_tensor(f"wub{i}", [P, 8 * 512], BF16) for i in range(2)]
            wdb = [ar.sbuf_tensor(f"wdb{i}", [P, 4 * D], BF16) for i in range(2)]
            sg_sb = ar.sbuf_tensor("sg_sb", [P, 512], F32)
            xq = sg_sb[:].bitcast(BF16)
            hbuf = psALL[:, 7 * 512:8 * 512]
            hq = ar.sbuf_tensor("hq", [P, 512], BF16)
            hqT = ar.sbuf_tensor("hqT", [P, 4 * P], BF16)
            ar.allocate()

            def ffn_body(ctx, g):
                NCK = 6
                xqT4 = xqT[:].rearrange("p (dm t c) -> p dm t c", t=4, c=P)
                ctx.wait(ctx.V, "dx")
                ctx.wait(ctx.S, "dx")
                xsl0 = stream[:, ds(g * (4 * D) + 0 * D, D)]
                emit_ln(ctx, LB, xsl0)
                for ti in range(4):
                    for hhf in range(2):
                        equant_v1(ctx, QB, LB["xlnt"][:, hhf * 512:(hhf + 1) * 512],
                                  512, sl=hhf)
                    equant_s(ctx, QB, 512, sl=0, slb=0)
                    if ti < 3:
                        xsl = stream[:, ds(g * (4 * D) + (ti + 1) * D, D)]
                        emit_ln(ctx, LB, xsl)
                    ctx.wait(ctx.V, "pe")
                    equant_v2(ctx, QB, xq[:, 0:512], 512, sl=0, slb=0)
                    equant_s(ctx, QB, 512, sl=1, slb=0)
                    equant_v2(ctx, QB, xq[:, 512:1024], 512, sl=1, slb=0)
                    ctx.wait(ctx.TE, "vv")
                    for dm in range(8):
                        gj = dm % 2
                        if gj == 0:
                            ctx.wait(ctx.TE, "ss")
                        w_ = ctx.TE.transpose(
                            out=bankb(6)[:, gj * P:(gj + 1) * P],
                            in_=xq[:, ts(dm, P)], identity=ident_b[:])
                        if gj == 1:
                            w_.then_inc("pe", 1)
                            ctx.wait(ctx.S, "pe")
                            ctx.S.activation(
                                out=xqT4[:, dm - 1:dm + 1, ti, :],
                                in_=bankb(6)[:, 0:2 * P].rearrange(
                                    "p (g c) -> p g c", c=P),
                                func=AF.Copy, bias=0.0,
                                scale=1.0).then_inc("ss", 1)
                    ctx.set_mark(("xev", ti), "ss")

                NIT = NCK * 4
                wgv = [wgb[i][:].rearrange("p (dc i) -> p dc i", i=512) for i in range(2)]
                wuv = [wub[i][:].rearrange("p (dc i) -> p dc i", i=512) for i in range(2)]
                wdv = [wdb[i][:].rearrange("p (ic d) -> p ic d", d=D) for i in range(2)]

                def mk(name, k):
                    return (name, k % NIT)

                def lagof(k):
                    return 1 if k < 0 else 0

                def wm(eng, name, k):
                    ctx.wait_mark(eng, ctx.get_mark(mk(name, k)), lag=lagof(k))

                def wload(c, lag=0):
                    par = c % 2
                    Ic = 512 if c < 5 else 256
                    k_free = (c - 2) * 4 + 3
                    wm(ctx.SY, "st0", k_free)
                    ctx.wait(ctx.SY, "dqw")
                    ctx.wait(ctx.SY, "dw")
                    ctx.SY.dma_start(
                        out=wgv[par][:, :, 0:Ic],
                        in_=qwg.rearrange("(dc p) i -> p dc i", p=P)[:, :, c * 512:c * 512 + Ic]
                        ).then_inc("dw", 16)
                    ctx.set_mark(("wbg", c), "dw")
                    ctx.wait(ctx.SY, "dw")
                    ctx.SY.dma_start(
                        out=wuv[par][:, :, 0:Ic],
                        in_=qwu.rearrange("(dc p) i -> p dc i", p=P)[:, :, c * 512:c * 512 + Ic]
                        ).then_inc("dw", 16)
                    ctx.set_mark(("wbu", c), "dw")

                def wdload(c):
                    par = c % 2
                    Ic = 512 if c < 5 else 256
                    nsub = Ic // P
                    k_free = (c - 2) * 4 + 3
                    wm(ctx.G, "st9", k_free)
                    ctx.wait(ctx.G, "dqw")
                    ctx.wait(ctx.G, "dw2")
                    ctx.G.dma_start(
                        out=wdv[par][:, 0:nsub, :],
                        in_=qwd.rearrange("(ic p) d -> p ic d", p=P)[:, 4 * c:4 * c + nsub, :]
                        ).then_inc("dw2", 16)

                def sizes(k):
                    c, ti = k // 4, k % 4
                    Ic = 512 if c < 5 else 256
                    return c, ti, Ic, Ic // P

                def st0(k):
                    c, ti, Ic, nsub = sizes(k)
                    ctx.wait_mark(ctx.TE, ctx.get_mark(("wbg", c)), lag=0)
                    if k < 2:
                        ctx.wait_mark(ctx.TE, ctx.get_mark(("xev", ti)), lag=0)
                    wm(ctx.TE, "silu", k - 2)
                    for dm in range(8):
                        w_ = ctx.TE.matmul(out=bank(k % 2, Ic),
                                           lhsT=xqT4[:, dm, ti, :],
                                           rhs=wgv[c % 2][:, dm, 0:Ic],
                                           start=(dm == 0), stop=(dm == 7))
                    ctx.wait_mark(ctx.TE, ctx.get_mark(("wbu", c)), lag=0)
                    wm(ctx.TE, "hm", k - 2)
                    for dm in range(8):
                        w_ = ctx.TE.matmul(out=bank(2 + k % 2, Ic),
                                           lhsT=xqT4[:, dm, ti, :],
                                           rhs=wuv[c % 2][:, dm, 0:Ic],
                                           start=(dm == 0), stop=(dm == 7))
                    w_.then_inc("pe", 1)
                    ctx.set_mark(mk("st0", k), "pe")

                def st1(k):
                    c, ti, Ic, nsub = sizes(k)
                    wm(ctx.S, "st0", k)
                    wm(ctx.S, "hm", k - 1)
                    ctx.S.activation(out=sg_sb[:, 0:Ic], in_=bank(k % 2, Ic),
                                     func=AF.Silu).then_inc("ss", 1)
                    ctx.set_mark(mk("silu", k), "ss")

                def st23(k):
                    c, ti, Ic, nsub = sizes(k)
                    wm(ctx.V, "silu", k)
                    ctx.V.tensor_tensor(out=hbuf[:, 0:Ic], in0=sg_sb[:, 0:Ic],
                                        in1=bank(2 + k % 2, Ic),
                                        op=ALU.mult).then_inc("vv", 1)
                    ctx.set_mark(mk("hm", k), "vv")
                    equant_v1(ctx, QB, hbuf[:, 0:Ic], Ic, sl=k % 2)
                    ctx.set_mark(mk("v1", k), "vv")

                def st4(k):
                    c, ti, Ic, nsub = sizes(k)
                    wm(ctx.S, "v1", k)
                    wm(ctx.S, "q", k - 2)
                    equant_s(ctx, QB, Ic, sl=k % 2, skip_wait=True)
                    ctx.set_mark(mk("sgn", k), "ss")

                def st6(k):
                    c, ti, Ic, nsub = sizes(k)
                    wm(ctx.V, "sgn", k)
                    wm(ctx.V, "st7", k - 1)
                    equant_v2(ctx, QB, hq[:, 0:Ic], Ic, sl=k % 2, skip_wait=True)
                    ctx.set_mark(mk("q", k), "vv")

                def st7(k):
                    c, ti, Ic, nsub = sizes(k)
                    wm(ctx.TE, "q", k)
                    wm(ctx.TE, "st8", k - 1)
                    w_ = None
                    for j in range(nsub):
                        w_ = ctx.TE.transpose(
                            out=bankb(6)[:, j * P:(j + 1) * P],
                            in_=hq[:, ts(j, P)], identity=ident_b[:])
                    w_.then_inc("pe", 1)
                    ctx.set_mark(mk("st7", k), "pe")

                def st8(k):
                    c, ti, Ic, nsub = sizes(k)
                    wm(ctx.S, "st7", k)
                    wm(ctx.S, "st9", k - 1)
                    ctx.S.activation(out=hqT[:, 0:nsub * P],
                                     in_=bankb(6)[:, 0:nsub * P],
                                     func=AF.Copy, bias=0.0,
                                     scale=1.0).then_inc("ss", 1)
                    ctx.set_mark(mk("st8", k), "ss")

                def st9(k):
                    c, ti, Ic, nsub = sizes(k)
                    ctx.wait_mark(ctx.TE,
                                  ctx.get_mark(("wbd", 0 if c < 2 else c)), lag=0)
                    wm(ctx.TE, "st8", k)
                    wm(ctx.TE, "st10", k - 1)
                    for oh in range(2):
                        for isub in range(nsub):
                            w_ = ctx.TE.matmul(out=bank(4 + oh),
                                               lhsT=hqT[:, ts(isub, P)],
                                               rhs=wdv[c % 2][:, isub, oh * 512:(oh + 1) * 512],
                                               start=(isub == 0),
                                               stop=(isub == nsub - 1))
                        w_.then_inc("pe", 1)
                    ctx.set_mark(mk("st9", k), "pe")

                def st10(k):
                    c, ti, Ic, nsub = sizes(k)
                    wm(ctx.V, "st9", k)
                    ctx.V.scalar_tensor_tensor(
                        out=stream[:, ds(g * (4 * D) + ti * D, 1024)],
                        in0=psALL[:, 4 * 512:6 * 512], scalar=0.5,
                        in1=stream[:, ds(g * (4 * D) + ti * D, 1024)],
                        op0=ALU.mult, op1=ALU.add).then_inc("vv", 1)
                    ctx.set_mark(mk("st10", k), "vv")

                wload(0)
                wload(1)
                wdload(0)
                wdload(1)
                ctx.set_mark(("wbd", 0), "dw2")
                for b in range(NIT + 5):
                    if b < NIT:
                        st0(b)
                    if 0 <= b - 3 < NIT:
                        st7(b - 3)
                    if 0 <= b - 4 < NIT:
                        st9(b - 4)
                    if 0 <= b - 3 < NIT:
                        st8(b - 3)
                    if 0 <= b - 1 < NIT:
                        st1(b - 1)
                        st4(b - 1)
                    if 0 <= b - 2 < NIT:
                        st6(b - 2)
                    if 0 <= b - 1 < NIT:
                        st23(b - 1)
                    if 0 <= b - 4 < NIT:
                        st10(b - 4)
                    if b % 4 == 0:
                        if 2 <= b // 4 + 1 <= NCK - 1:
                            wload(b // 4 + 1)
                        if 2 <= b // 4 + 2 <= NCK - 1:
                            wdload(b // 4 + 2)
                            ctx.set_mark(("wbd", b // 4 + 2), "dw2")

            SS.run_loop(nc, 0, NG, ffn_body)
            arena_fence()
            ar.free()

        if do_ffn1:
            emit_ffn_phase()

        if do_attn:
            qkv_ar = nc.auto_arena()
            LBq = make_ln_bufs(qkv_ar)
            xlnT = qkv_ar.sbuf_tensor("xlnT", [P, 32 * P], BF16)
            wkb = [qkv_ar.sbuf_tensor(f"wkb{i}", [P, 8 * 512], BF16) for i in range(2)]
            Vev = qkv_ar.sbuf_tensor("Vev", [P, 4 * VW], BF16)
            qkev = qkv_ar.sbuf_tensor("qkev", [P, 512], BF16)
            qkv_ar.allocate()

            cq = SS.straight(nc)
            cq.wait(cq.G, "dx")
            cq.G.memset(Vev[:].rearrange("p (t h c) -> p t h c", h=NH, c=HD + 1)
                        [:, :, :, HD:HD + 1], 1.0).then_inc("gg", 1)
            SS.flush_straight()

            def qkv_body(ctx, g):
                xlnT4 = xlnT[:].rearrange("p (dm t c) -> p dm t c", t=4, c=P)
                ctx.wait(ctx.V, "dx")
                ctx.wait(ctx.S, "dx")
                for ti in range(4):
                    xsl = stream[:, ds(g * (4 * D) + ti * D, D)]
                    emit_ln(ctx, LBq, xsl)
                    ctx.wait(ctx.TE, "vv")
                    for dm in range(8):
                        gj = dm % 2
                        if gj == 0:
                            ctx.wait(ctx.TE, "ss")
                        w_ = ctx.TE.transpose(
                            out=bank(6)[:, gj * P:(gj + 1) * P],
                            in_=LBq["xlnt"][:, ts(dm, P)], identity=ident_f[:])
                        if gj == 1:
                            w_.then_inc("pe", 1)
                            ctx.wait(ctx.S, "pe")
                            ctx.S.activation(
                                out=xlnT4[:, dm - 1:dm + 1, ti, :],
                                in_=bank(6)[:, 0:2 * P].rearrange(
                                    "p (g c) -> p g c", c=P),
                                func=AF.Copy, bias=0.0,
                                scale=1.0).then_inc("ss", 1)
                for ck in range(6):
                    par = ck % 2
                    ctx.wait(ctx.SY, "pe")
                    ctx.wait(ctx.SY, "dw2")
                    ctx.wait(ctx.SY, "dk")
                    ctx.SY.dma_start(
                        out=wkb[par][:].rearrange("p (dc o) -> p dc o", o=512),
                        in_=wqkvb.rearrange("(dc p) o -> p dc o", p=P)
                        [:, :, ck * 512:(ck + 1) * 512]).then_inc("dk", 16)
                    if ck < 4:
                        for oj in range(4):
                            bk = oj % 2
                            ctx.wait(ctx.TE, "dk")
                            ctx.wait(ctx.TE, "ss")
                            for dc in range(8):
                                w_ = ctx.TE.matmul(
                                    out=bank(bk),
                                    lhsT=wkb[par][:].rearrange(
                                        "p (dc o) -> p dc o", o=512)[:, dc, oj * P:(oj + 1) * P],
                                    rhs=xlnT[:, ts(dc, 512)],
                                    start=(dc == 0), stop=(dc == 7))
                            w_.then_inc("pe", 1)
                            ctx.wait(ctx.S, "pe")
                            ctx.wait(ctx.S, "dq")
                            ctx.S.activation(out=qkev[:], in_=bank(bk),
                                             func=AF.Copy, bias=0.0,
                                             scale=(0.125 if ck < 2 else 1.0)
                                             ).then_inc("ss", 1)
                            ctx.wait(ctx.SY, "ss")
                            dst = Qs if ck < 2 else Ks
                            rr = (ck % 2) * 4 + oj
                            ctx.SY.dma_start(
                                out=dst.rearrange("(r p) t -> p r t", p=P)[:, rr, ds(g * 512, 512)],
                                in_=qkev[:]).then_inc("dq", 16)
                    else:
                        for ti in range(4):
                            bk = ti % 2
                            ctx.wait(ctx.TE, "dk")
                            ctx.wait(ctx.TE, "ss")
                            ctx.wait(ctx.TE, "dq")
                            for dc in range(8):
                                w_ = ctx.TE.matmul(
                                    out=bank(bk),
                                    lhsT=xlnT4[:, dc, ti, :],
                                    rhs=wkb[par][:].rearrange(
                                        "p (dc o) -> p dc o", o=512)[:, dc, :],
                                    start=(dc == 0), stop=(dc == 7))
                            w_.then_inc("pe", 1)
                            ctx.wait(ctx.S, "pe")
                            ctx.wait(ctx.S, "dq")
                            hoff = (ck - 4) * 8 * (HD + 1)
                            ctx.S.activation(
                                out=Vev[:, ti * VW + hoff:ti * VW + hoff + 8 * (HD + 1)]
                                .rearrange("p (h c) -> p h c", c=HD + 1)[:, :, 0:HD],
                                in_=bank(bk).rearrange("p (h c) -> p h c", c=HD),
                                func=AF.Copy, bias=0.0, scale=1.0).then_inc("ss", 1)
                ctx.wait(ctx.SY, "ss")
                ctx.SY.dma_start(
                    out=Vs.rearrange("(b p) c -> p b c", p=P)[:, ds(g * 4, 4), :],
                    in_=Vev[:].rearrange("p (b c) -> p b c", c=VW)
                    ).then_inc("dq", 16)

            SS.run_loop(nc, 0, NG, qkv_body)
            arena_fence()
            qkv_ar.free()

            core_ar = nc.auto_arena()
            qbuf = core_ar.sbuf_tensor("qbuf", [65, 2 * 4 * 512], BF16)
            KbD = core_ar.sbuf_tensor("KbD", [65, 4 * 512], BF16)
            VbDg = core_ar.sbuf_tensor("VbDg", [P, 16 * (HD + 1)], BF16)
            Kb = core_ar.sbuf_tensor("Kb", [65, 2 * 4 * 512], BF16)
            Vb = core_ar.sbuf_tensor("Vb", [P, 2 * 16 * (HD + 1)], BF16)
            eT2 = core_ar.sbuf_tensor("eT2", [P, 2 * 4 * 512], BF16)
            wpTs = core_ar.sbuf_tensor("wpTs", [P, 8 * D], BF16)
            Oqt = core_ar.sbuf_tensor("Oqt", [P, 8 * 512], BF16)
            Ostg = core_ar.sbuf_tensor("Ostg", [64, 512], BF16)
            ABa = core_ar.sbuf_tensor("ABa", [P, 512], F32)
            ABb = core_ar.sbuf_tensor("ABb", [64, 512], F32)
            otmp = core_ar.sbuf_tensor("otmp", [64, 512], F32)
            mcol = core_ar.sbuf_tensor("mcol", [P, 18], F32)
            rowAB = core_ar.sbuf_tensor("rowAB", [1, 2 * 4 * 512], BF16)
            Vall = core_ar.sbuf_tensor("Vall", [HD, NH], F32)
            onecol = core_ar.sbuf_tensor("onecol", [P, 1], BF16)
            core_ar.allocate()
            aparv = otmp[:].bitcast(BF16)[0:4, 0:512]
            bparv = otmp[:].bitcast(BF16)[0:4, 512:1024]
            Eparv = otmp[0:4, :]
            Dparv = ABa[0:4, :]
            Rparv = ABb[0:4, :]
            Vrowv = Oqt[:].bitcast(F32)[0:1, 0:VW]
            Vtilev = Oqt[:, 2080:2080 + VW]

            QsV = Qs.rearrange("(h d) t -> d h t", d=HD)
            KsV = Ks.rearrange("(h d) t -> d h t", d=HD)
            VsV = Vs.rearrange("(kt p) c -> p kt c", p=P)

            cva = SS.straight(nc)
            cva.wait(cva.G, "dq")
            cva.G.memset(onecol[:], 1.0).then_inc("gg", 1)
            cva.wait(cva.TE, "gg")
            w_ = None
            for bk in (5, 6, 7):
                w_ = cva.TE.matmul(out=psALL[0:1, bk * 512:(bk + 1) * 512],
                                   lhsT=zero_row[:, 0:1],
                                   rhs=ones_row[:], start=True, stop=False)
            w_.then_inc("pe", 1)
            SS.flush_straight()

            def vall_body(ctx, t):
                ctx.wait(ctx.SY, "pe")
                ctx.SY.dma_start(out=Vtilev,
                                 in_=Vs.rearrange("(b p) c -> p b c", p=P)[:, t, :]
                                 ).then_inc("dkv", 16)
                ctx.wait(ctx.TE, "dkv")
                ctx.TE.matmul(out=psALL[0:1, 6 * 512:7 * 512], lhsT=onecol[:],
                              rhs=Vtilev[:, 0:512], start=False, stop=False)
                ctx.TE.matmul(out=psALL[0:1, 7 * 512:8 * 512], lhsT=onecol[:],
                              rhs=Vtilev[:, 512:1024], start=False, stop=False)
                w2 = ctx.TE.matmul(out=psALL[0:1, 5 * 512:5 * 512 + 16],
                                   lhsT=onecol[:], rhs=Vtilev[:, 1024:1040],
                                   start=False, stop=False)
                w2.then_inc("pe", 1)

            SS.run_loop(nc, 0, NT, vall_body)
            ccl = SS.straight(nc)
            w_ = None
            for bk in (5, 6, 7):
                w_ = ccl.TE.matmul(out=psALL[0:1, bk * 512:(bk + 1) * 512],
                                   lhsT=zero_row[:, 0:1],
                                   rhs=ones_row[:], start=False, stop=True)
            w_.then_inc("pe", 1)
            SS.flush_straight()

            cvt = SS.straight(nc)
            cvt.wait(cvt.V, "pe")
            cvt.V.tensor_copy(out=Vrowv[0:1, 0:512],
                              in_=psALL[0:1, 6 * 512:7 * 512]).then_inc("vv", 1)
            cvt.V.tensor_copy(out=Vrowv[0:1, 512:1024],
                              in_=psALL[0:1, 7 * 512:8 * 512]).then_inc("vv", 1)
            cvt.V.tensor_copy(out=Vrowv[0:1, 1024:1040],
                              in_=psALL[0:1, 5 * 512:5 * 512 + 16]).then_inc("vv", 1)
            cvt.wait(cvt.SY, "vv")
            cvt.SY.dma_start(out=vascr.rearrange("(o f) -> o f", o=1)[:, 0:512],
                             in_=Vrowv[:, 0:512]).then_inc("dkv", 16)
            cvt.SY.dma_start(out=vascr.rearrange("(o f) -> o f", o=1)[:, 512:1024],
                             in_=Vrowv[:, 512:1024]).then_inc("dkv", 16)
            cvt.SY.dma_start(out=vascr.rearrange("(o f) -> o f", o=1)[:, 1024:1040],
                             in_=Vrowv[:, 1024:1040]).then_inc("dkv", 16)
            cvt.wait(cvt.TE, "vv")
            cvt.wait(cvt.TE, "pe")
            w_ = None
            for h in range(NH):
                w_ = cvt.TE.transpose(out=psALL[0:HD, h:h + 1],
                                      in_=Vrowv[0:1, h * (HD + 1):h * (HD + 1) + HD],
                                      identity=ident_f[0:1, 0:1])
            w_.then_inc("pe", 1)
            cvt.wait(cvt.V, "pe")
            cvt.V.tensor_copy(out=Vall[:, :],
                              in_=psALL[0:HD, 0:NH]).then_inc("vv", 1)
            cvt.G.dma_start(out=wpTs[:].rearrange("p (dc o) -> p dc o", o=D),
                            in_=wpT_in.rearrange("(dc p) o -> p dc o", p=P)
                            ).then_inc("dw2", 16)
            cvt.G.memset(Kb[64:65, :], 1.0).then_inc("gg", 1)
            cvt.G.memset(KbD[64:65, :], 1.0).then_inc("gg", 1)
            cvt.wait(cvt.G, "dq")
            cvt.wait(cvt.G, "dkv")
            cvt.G.dma_start(
                out=qbuf[0:HD, 0:2048].rearrange("d (h t) -> d h t", t=512),
                in_=QsV[:, 0:4, 0:512]).then_inc("dw2", 16)
            cvt.G.dma_start(
                out=KbD[0:HD, :].rearrange("d (h t) -> d h t", t=512),
                in_=KsV[:, 0:4, 0:512]).then_inc("dw2", 16)
            SS.flush_straight()

            def emit_epi_head(ctx, p):
                ctx.wait(ctx.S, "pe")
                for hh in range(4):
                    ctx.wait(ctx.S, "dq")
                    ctx.S.activation(
                        out=ABa[64:65, :],
                        in_=psALL[64:65, (4 + hh) * 512:(5 + hh) * 512],
                        func=AF.Copy, bias=0.0, scale=1.0).then_inc("ss", 1)
                    ctx.wait(ctx.S, "ss")
                    ctx.S.dma_start(
                        out=escr.rearrange("(o f) -> o f", o=1)
                        [0:1, hh * 512:(hh + 1) * 512],
                        in_=ABa[64:65, :]).then_inc("dq", 16)
                ctx.wait(ctx.S, "dq")
                ctx.wait(ctx.S, "vv")
                ctx.S.dma_start(out=Eparv,
                                in_=escr.rearrange("(h f) -> h f", h=4)
                                ).then_inc("dq", 16)
                c1 = 1.0 + float(T) * 1e-8
                c2 = float(T) * 1e-14
                ctx.wait(ctx.V, "dq")
                ctx.V.tensor_scalar(out=Dparv, in0=Eparv, scalar1=c1,
                                    scalar2=c2, op0=ALU.mult,
                                    op1=ALU.add).then_inc("vv", 1)
                ctx.wait(ctx.V, "vv")
                ctx.V.reciprocal(out=Rparv, in_=Dparv).then_inc("vv", 1)
                ctx.wait(ctx.V, "vv")
                ctx.V.tensor_scalar(out=Dparv, in0=Eparv, scalar1=1e-6,
                                    scalar2=None, op0=ALU.add).then_inc("vv", 1)
                ctx.wait(ctx.V, "vv")
                ctx.V.tensor_copy(out=aparv, in_=Rparv).then_inc("vv", 1)
                ctx.wait(ctx.V, "vv")
                ctx.V.scalar_tensor_tensor(out=bparv, in0=Dparv, scalar=1e-8,
                                           in1=Rparv, op0=ALU.mult,
                                           op1=ALU.mult).then_inc("vv", 1)
                ctx.wait(ctx.S, "vv")
                ctx.wait(ctx.S, "pe")
                ctx.S.dma_start(out=abscr[0].rearrange("(h f) -> h f", h=4),
                                in_=aparv).then_inc("dq", 16)
                ctx.S.dma_start(out=abscr[1].rearrange("(h f) -> h f", h=4),
                                in_=bparv).then_inc("dq", 16)
                ctx.wait(ctx.S, "dq")
                ctx.S.dma_start(
                    out=rowAB[:],
                    in_=abscr.rearrange("r f -> (r f)").rearrange(
                        "(o f) -> o f", o=1)).then_inc("dq", 16)

            def emit_epi_tail(ctx, p):
                for hh in range(4):
                    h = 4 * p + hh
                    ctx.wait(ctx.TE, "dq")
                    ctx.wait(ctx.TE, "ss")
                    w_ = ctx.TE.matmul(out=bank(2)[0:HD, :],
                                       lhsT=ones_row[0:1, 0:HD],
                                       rhs=rowAB[0:1, hh * 512:(hh + 1) * 512],
                                       start=True, stop=True)
                    w_ = ctx.TE.matmul(
                        out=bank(3)[0:HD, :],
                        lhsT=ones_row[0:1, 0:HD],
                        rhs=rowAB[0:1, 2048 + hh * 512:2048 + (hh + 1) * 512],
                        start=True, stop=True)
                    w_.then_inc("pe", 1)
                    ctx.wait(ctx.S, "pe")
                    ctx.wait(ctx.S, "vv")
                    ctx.S.activation(out=ABa[0:HD, :], in_=bank(2)[0:HD, :],
                                     func=AF.Copy, bias=0.0,
                                     scale=1.0).then_inc("ss", 1)
                    ctx.S.activation(out=ABb[0:HD, :], in_=bank(3)[0:HD, :],
                                     func=AF.Copy, bias=0.0,
                                     scale=1.0).then_inc("ss", 1)
                    ctx.wait(ctx.V, "ss")
                    ctx.wait(ctx.V, "st")
                    if hh == 0:
                        ctx.wait(ctx.V, "dq")
                    ctx.V.tensor_tensor(out=otmp[:], in0=bank(4 + hh)[0:HD, :],
                                        in1=ABa[0:HD, :],
                                        op=ALU.mult).then_inc("vv", 1)
                    ctx.wait(ctx.V, "vv")
                    ctx.V.scalar_tensor_tensor(
                        out=Ostg[:], in0=ABb[0:HD, :],
                        scalar=Vall[:, h:h + 1], in1=otmp[:],
                        op0=ALU.mult, op1=ALU.add).then_inc("vv", 1)
                    ctx.wait(ctx.SY, "vv")
                    ctx.wait(ctx.SY, "pe")
                    ctx.SY.dma_start(
                        out=Oqt[(h % 2) * HD:(h % 2 + 1) * HD,
                                (h // 2) * 512:(h // 2 + 1) * 512],
                        in_=Ostg[:]).then_inc("st", 16)

            def emit_diag_prep(ctx, p, qt, after_epi):
                qoff = (p % 2) * 2048
                for j in range(18):
                    if j >= 2:
                        jj = j - 2
                        hh, qsub = divmod(jj, 4)
                        wcols = (qsub + 1) * P
                        ctx.wait(ctx.V, "pe")
                        ctx.wait(ctx.V, "vv")
                        ctx.V.tensor_tensor(
                            out=bank(jj % 2)[:, qsub * P:wcols],
                            in0=bank(jj % 2)[:, qsub * P:wcols],
                            in1=trimask[:], op=ALU.add).then_inc("vv", 1)
                        ctx.wait(ctx.V, "vv")
                        ctx.V.tensor_reduce(
                            out=mcol[:, jj:jj + 1], in_=bank(jj % 2)[:, 0:wcols],
                            axis=AX.X, op=ALU.max,
                            negate=True).then_inc("vv", 1)
                        ctx.wait(ctx.V, "vv")
                        ctx.V.tensor_scalar(out=mcol[:, jj:jj + 1],
                                            in0=mcol[:, jj:jj + 1],
                                            scalar1=100.0, scalar2=-100.0,
                                            op0=ALU.min,
                                            op1=ALU.max).then_inc("vv", 1)
                        ctx.wait(ctx.TE, "vv")
                        ctx.wait(ctx.TE, "pe")
                        ctx.wait(ctx.TE, "ss")
                        w_ = ctx.TE.transpose(out=bank(2 + jj % 2)[0:1, 0:P],
                                              in_=mcol[:, jj:jj + 1],
                                              identity=ident_f[:])
                        w_.then_inc("pe", 1)
                        ctx.wait(ctx.S, "pe")
                        ctx.S.activation(
                            out=qbuf[64:65, qoff + hh * 512 + qsub * P:
                                     qoff + hh * 512 + (qsub + 1) * P],
                            in_=bank(2 + jj % 2)[0:1, 0:P], func=AF.Copy,
                            bias=0.0, scale=1.0).then_inc("ss", 1)
                    if j < 16:
                        hh, qsub = divmod(j, 4)
                        wcols = (qsub + 1) * P
                        if j < 2:
                            ctx.wait(ctx.TE, "dw2")
                            ctx.wait(ctx.TE, "ss")
                            ctx.wait(ctx.TE, "vv", back=5 if after_epi else 0)
                        else:
                            ctx.wait(ctx.TE, "vv")
                        w_ = ctx.TE.matmul(
                            out=bank(j % 2)[:, 0:wcols],
                            lhsT=qbuf[0:HD, qoff + hh * 512 + qsub * P:
                                      qoff + hh * 512 + (qsub + 1) * P],
                            rhs=KbD[0:HD, hh * 512:hh * 512 + wcols],
                            start=True, stop=True)
                        w_.then_inc("pe", 1)

            def emit_scores_exp(ctx, p, ktl, kbv, diag):
                qoff = (p % 2) * 2048
                ep = ktl % 2
                lo = ktl * P if diag else 0
                for pr in range(2):
                    h0 = 2 * pr
                    ctx.wait(ctx.TE, "ss", back=1)
                    for hh in (h0, h0 + 1):
                        w_ = ctx.TE.matmul(
                            out=bank(hh),
                            lhsT=kbv[:, ds(hh * 512 + ktl * P, P)],
                            rhs=qbuf[:, qoff + hh * 512:qoff + (hh + 1) * 512],
                            start=True, stop=True)
                    w_.then_inc("pe", 1)
                    if diag:
                        ctx.wait(ctx.V, "pe")
                        for hh in (h0, h0 + 1):
                            ctx.V.tensor_tensor(
                                out=bank(hh)[:, ktl * P:(ktl + 1) * P],
                                in0=bank(hh)[:, ktl * P:(ktl + 1) * P],
                                in1=trimaskT[:], op=ALU.add).then_inc("vv", 1)
                        ctx.wait(ctx.S, "vv")
                    else:
                        ctx.wait(ctx.S, "pe")
                    expin = psALL[:, h0 * 512:(h0 + 2) * 512].rearrange(
                        "p (h q) -> p h q", q=512)[:, :, lo:512]
                    expout = eT2[:, ds(ep * 2048 + h0 * 512, 1024)].rearrange(
                        "p (h q) -> p h q", q=512)[:, :, lo:512]
                    ctx.S.activation(out=expout, in_=expin, func=AF.Exp,
                                     bias=0.0, scale=1.0).then_inc("ss", 1)

            def emit_av(ctx, p, ktl, vbv, diag, stop, ss_back):
                ep = ktl % 2
                lo = ktl * P if diag else 0
                for pr in range(2):
                    h0 = 2 * pr
                    ctx.wait(ctx.TE, "ss", back=ss_back - pr)
                    for hh in (h0, h0 + 1):
                        w_ = ctx.TE.matmul(
                            out=bank(4 + hh)[0:HD + 1, lo:512],
                            lhsT=vbv[:, ds((ktl * 4 + hh) * (HD + 1), HD + 1)],
                            rhs=eT2[:, ds(ep * 2048 + hh * 512 + lo, 512 - lo)],
                            start=False, stop=stop)
                    w_.then_inc("pe", 1)

            def emit_chunk(ctx, p, kbv, vbv, diag, dkv_back):
                for ktl in range(4):
                    if ktl == 0:
                        ctx.wait(ctx.TE, "dkv", back=dkv_back)
                    emit_scores_exp(ctx, p, ktl, kbv, diag)
                    if ktl > 0:
                        emit_av(ctx, p, ktl - 1, vbv, diag, stop=False,
                                ss_back=3)
                emit_av(ctx, p, 3, vbv, diag, stop=diag, ss_back=1)

            def load_chunk(ctx2, p, cidx, par, inc=True):
                ctx2.SY.dma_start(
                    out=Kb[0:HD, par * 2048:(par + 1) * 2048].rearrange(
                        "d (h t) -> d h t", t=512),
                    in_=KsV[:, 4 * p:4 * p + 4, ds(cidx * 512, 512)]
                    ).then_inc("dkv", 16)
                ctx2.SY.dma_start(
                    out=Vb[:, par * 1040:(par + 1) * 1040].rearrange(
                        "p (kt h c) -> p kt h c", h=4, c=HD + 1),
                    in_=VsV[:, ds(cidx * 4, 4), :].rearrange(
                        "p kt (h c) -> p kt h c", c=HD + 1)
                    [:, :, 4 * p:4 * p + 4, :]).then_inc("dkv", 16)

            def make_pair_body(p):
                def pair_body(ctx2, j):
                    emit_chunk(ctx2, p, Kb[:, 0:2048], Vb[:, 0:1040],
                               diag=False, dkv_back=32)
                    ctx2.wait(ctx2.SY, "pe")
                    ctx2.wait(ctx2.SY, "dkv")
                    load_chunk(ctx2, p, (2 * j + 2) % NG, 0)
                    emit_chunk(ctx2, p, Kb[:, 2048:4096], Vb[:, 1040:2080],
                               diag=False, dkv_back=32)
                    ctx2.wait(ctx2.SY, "pe")
                    ctx2.wait(ctx2.SY, "dkv")
                    load_chunk(ctx2, p, (2 * j + 3) % NG, 1)
                return pair_body

            def make_tail_body(p):
                def tail_body(ctx2, t):
                    emit_chunk(ctx2, p, Kb[:, 0:2048], Vb[:, 0:1040],
                               diag=False, dkv_back=0)
                return tail_body

            def core_body(ctx, qt):
                ctx.wait(ctx.TE, "gg")
                ctx.wait(ctx.TE, "dw2")
                for p in range(4):
                    if p > 0:
                        emit_epi_head(ctx, p - 1)
                    ctx.wait(ctx.SY, "pe")
                    ctx.wait(ctx.SY, "dkv")
                    load_chunk(ctx, p, 0, 0)
                    ctx.SY.dma_start(
                        out=VbDg[:].rearrange(
                            "p (kt h c) -> p kt h c", h=4, c=HD + 1),
                        in_=VsV[:, ds(qt * 4, 4), :].rearrange(
                            "p kt (h c) -> p kt h c", c=HD + 1)
                        [:, :, 4 * p:4 * p + 4, :]).then_inc("dkv", 16)
                    ctx.wait(ctx.SY, "dkv")
                    load_chunk(ctx, p, 1 % NG, 1)
                    emit_diag_prep(ctx, p, qt, after_epi=(p > 0))
                    pn = (p + 1) % 4
                    qq = qt if p < 3 else (qt + 1) % NG
                    ctx.wait(ctx.G, "dw2")
                    ctx.wait(ctx.G, "pe")
                    ctx.G.dma_start(
                        out=qbuf[0:HD, ds((pn % 2) * 2048, 2048)].rearrange(
                            "d (h t) -> d h t", t=512),
                        in_=QsV[:, 4 * pn:4 * pn + 4, ds(qq * 512, 512)]
                        ).then_inc("dw2", 16)
                    if p > 0:
                        emit_epi_tail(ctx, p - 1)
                    ctx.wait(ctx.TE, "vv")
                    ctx.wait(ctx.TE, "ss")
                    ctx.wait(ctx.TE, "dq")
                    ctx.wait(ctx.TE, "pe")
                    w_ = None
                    for hh in range(4):
                        w_ = ctx.TE.matmul(
                            out=psALL[0:HD + 1, (4 + hh) * 512:(5 + hh) * 512],
                            lhsT=zero_row[:, 0:HD + 1],
                            rhs=ones_row[:], start=True, stop=False)
                    w_.then_inc("pe", 1)
                    ctx.inner(qt // 2, make_pair_body(p))
                    ctx.inner(qt % 2, make_tail_body(p))
                    emit_chunk(ctx, p, KbD, VbDg, diag=True, dkv_back=0)
                    ctx.wait(ctx.G, "dw2")
                    ctx.wait(ctx.G, "pe")
                    ctx.G.dma_start(
                        out=KbD[0:HD, :].rearrange("d (h t) -> d h t", t=512),
                        in_=KsV[:, 4 * pn:4 * pn + 4, ds(qq * 512, 512)]
                        ).then_inc("dw2", 16)
                emit_epi_head(ctx, 3)
                emit_epi_tail(ctx, 3)

                for qsub in range(4):
                    for oh in range(2):
                        gi = qsub * 2 + oh
                        bk = gi % 2 + 2
                        ctx.wait(ctx.TE, "st")
                        ctx.wait(ctx.TE, "vv", back=1 if gi > 0 else 0)
                        ctx.wait(ctx.TE, "ss")
                        for dc in range(8):
                            w_ = ctx.TE.matmul(
                                out=bank(bk),
                                lhsT=Oqt[:, dc * 512 + qsub * P:dc * 512 + (qsub + 1) * P],
                                rhs=wpTs[:].rearrange("p (dc o) -> p dc o", o=D)
                                [:, dc, oh * 512:(oh + 1) * 512],
                                start=(dc == 0), stop=(dc == 7))
                        w_.then_inc("pe", 1)
                        ctx.wait(ctx.V, "pe")
                        ctx.V.scalar_tensor_tensor(
                            out=stream[:, ds(qt * (4 * D) + qsub * D + oh * 512, 512)],
                            in0=bank(bk), scalar=1.0,
                            in1=stream[:, ds(qt * (4 * D) + qsub * D + oh * 512, 512)],
                            op0=ALU.mult, op1=ALU.add).then_inc("vv", 1)
                ctx.wait(ctx.TE, "vv")

            def cum_pairs(gv):
                return (gv * gv - 2 * gv + (gv % 2)) // 4
            def cum_tail(gv):
                return gv // 2
            icums = [(cum_pairs, sum(q // 2 for q in range(NG))),
                     (cum_tail, sum(q % 2 for q in range(NG)))] * 4
            SS.run_loop(nc, 0, NG, core_body, inner_cums=icums)
            arena_fence()
            core_ar.free()

        if do_ffn2:
            emit_ffn_phase()

        cf = SS.straight(nc)
        cf.wait(cf.SY, "vv")
        cf.wait(cf.SY, "pe")
        cf.SY.dma_start(out=out.rearrange("(t p) d -> p t d", p=P),
                        in_=stream[:, 0:NT * D].rearrange("p (t d) -> p t d", d=D)
                        ).then_inc("st", 16)
        cf.wait(cf.SY, "st")
        SS.flush_straight()

    return nc


def host_prepare(L0_attn_x, L0_attn_w_qkv, L0_attn_w_proj,
                 L0_ffn_w_gate, L0_ffn_w_up, L0_ffn_w_down):
    wqkvT = np.ascontiguousarray(np.asarray(L0_attn_w_qkv, dtype=np.float32).T)
    wpT = np.ascontiguousarray(np.asarray(L0_attn_w_proj, dtype=np.float32).T)
    wg = np.ascontiguousarray(np.asarray(L0_ffn_w_gate, dtype=np.float32))
    wu = np.ascontiguousarray(np.asarray(L0_ffn_w_up, dtype=np.float32))
    wd = np.ascontiguousarray(np.asarray(L0_ffn_w_down, dtype=np.float32))
    x = np.asarray(L0_attn_x, dtype=np.float32)
    return x, wqkvT, wpT, wg, wu, wd


def kernel(L0_attn_x, L0_attn_w_qkv, L0_attn_w_proj,
           L0_ffn_w_gate, L0_ffn_w_up, L0_ffn_w_down):
    x, wqkvT, wpT, wg, wu, wd = host_prepare(
        L0_attn_x, L0_attn_w_qkv, L0_attn_w_proj,
        L0_ffn_w_gate, L0_ffn_w_up, L0_ffn_w_down)
    B, T, _ = x.shape
    nc = build_nc(T=T, do_ffn1=True, do_attn=True, do_ffn2=True)
    nc.compile()
    in_maps = [{"x": np.ascontiguousarray(x[b]), "wg": wg, "wu": wu, "wd": wd,
                "wqkvT": wqkvT, "wpT": wpT} for b in range(B)]
    res = run_bass_kernel_spmd(nc, in_maps, core_ids=list(range(8)))
    return np.stack([np.asarray(r["out"]) for r in res.results], axis=0)


# version 20
# speedup vs baseline: 1.1876x; 1.1876x over previous
import numpy as np
import concourse.bass as bass
import concourse.bacc as bacc_mod
import concourse.mybir as mybir
from concourse.bass_utils import run_bass_kernel_spmd
from contextlib import ExitStack

F32 = mybir.dt.float32
BF16 = mybir.dt.bfloat16
I32 = mybir.dt.int32
AF = mybir.ActivationFunctionType
ALU = mybir.AluOpType
AX = mybir.AxisListType
ts = bass.ts
ds = bass.ds

P = 128
D = 1024
I_FFN = 2816
NH = 16
HD = 64
VW = NH * (HD + 1)
C1 = float(3 * 2 ** 21)
C2 = float(3 * 2 ** 22)
C3 = float(3 * 2 ** 23)
NEG = -1.0e9


from contextlib import contextmanager


@contextmanager
def Forw(nc, start, end):
    from concourse.bass import sinput_min_value, sinput_max_value
    engines = mybir.ALL_ENGINES
    loop_id = nc.next_id()
    loop_start = f"forw_{loop_id}_loop"
    loop_end = f"forw_{loop_id}_end"
    registers = nc.alloc_registers(f"forw_{loop_id}_i", engines=engines)
    nc.regs_mov(registers, start)
    nc.br_lt(registers, end, on_true=loop_start, on_false=loop_end,
             engines=engines)
    min_val = sinput_min_value(start)
    max_val = sinput_max_value(end)
    if max_val is not None:
        max_val -= 1
    with nc.body(loop_start, valid_engines=engines):
        yield nc.snap(registers, min_val=min_val, max_val=max_val)
        nc.regs_alu(registers, registers, 1, op=mybir.AluOpType.add)
        nc.br_lt(registers, end, on_true=loop_start, on_false=loop_end,
                 engines=engines)
    nc.switch_bb(loop_end)


class _Sem:
    def __init__(self, handle):
        self.h = handle
        self.pre = 0
        self.per = None
        self.cur = 0


class _InstStub:
    def __init__(self, ctx):
        self.ctx = ctx

    def then_inc(self, semname, n=1):
        self.ctx._count(semname, n)
        return self


class _InstWrap:
    def __init__(self, ctx, inst):
        self.ctx = ctx
        self.inst = inst

    def then_inc(self, semname, n=1):
        s = self.ctx.sems[semname]
        self.inst.then_inc(s.h, n)
        self.ctx._count(semname, n)
        return self


class _EngProxy:
    def __init__(self, ctx, eng):
        self._ctx = ctx
        self._eng = eng

    def __getattr__(self, name):
        ctx = self._ctx
        eng = self._eng

        def call(*a, **k):
            if ctx.counting:
                return _InstStub(ctx)
            return _InstWrap(ctx, getattr(eng, name)(*a, **k))
        return call


class Ctx:
    def __init__(self, nc, sems, counting, inner_reps=0):
        self.nc = nc
        self.sems = sems
        self.counting = counting
        self.inner_reps = inner_reps
        self.inner_pers_list = []
        self.off = {}
        self._per = {}
        self.marks_out = {}
        self.premarks = {}
        self._inner_idx = 0
        self.TE = _EngProxy(self, nc.tensor)
        self.V = _EngProxy(self, nc.vector)
        self.S = _EngProxy(self, nc.scalar)
        self.G = _EngProxy(self, nc.gpsimd)
        self.SY = _EngProxy(self, nc.sync)

    def _count(self, semname, n):
        self.sems[semname].cur += n

    def wait(self, engproxy, semname, back=0):
        if self.counting:
            return
        s = self.sems[semname]
        val = s.pre + s.cur - back
        off = self.off.get(semname, 0)
        expr = off + val if not isinstance(off, int) or off else val
        if isinstance(expr, int):
            assert expr >= 0, f"negative wait on {semname}"
        engproxy._eng.wait_ge(s.h, expr)

    def mark(self, semname):
        return (semname, self.sems[semname].cur)

    def set_mark(self, key, semname):
        self.marks_out[key] = (semname, self.sems[semname].cur)

    def get_mark(self, key):
        if key in self.marks_out:
            return self.marks_out[key]
        return self.premarks.get(key)

    def wait_mark(self, engproxy, mk, lag=0):
        if self.counting or mk is None:
            return
        semname, rel = mk
        s = self.sems[semname]
        val = s.pre + rel - lag * self._per.get(semname, 0)
        off = self.off.get(semname, 0)
        expr = off + val if not isinstance(off, int) or off else val
        if isinstance(expr, int):
            assert expr >= 0, f"negative mark-wait on {semname}"
        engproxy._eng.wait_ge(s.h, expr)

    def inner(self, trip, fn):
        if self.counting:
            snap = {n: s.cur for n, s in self.sems.items()}
            for _ in range(self.inner_reps):
                fn(self, 0)
            pers = {n: s.cur - snap[n] for n, s in self.sems.items()}
            self.inner_pers_list.append(pers)
            return
        if isinstance(trip, int) and trip == 0:
            if self.inner_pers_list:
                self._inner_idx += 1
            return
        pers = self.inner_pers_list[self._inner_idx]
        self._inner_idx += 1
        if isinstance(trip, int):
            for kc in range(trip):
                sub = Ctx(self.nc, self.sems, counting=False)
                sub.off = {}
                curs = {}
                for n, s in self.sems.items():
                    o = self.off.get(n, 0)
                    extra = kc * pers[n]
                    sub.off[n] = (o + extra) if not (isinstance(o, int) and o == 0 and extra == 0) else 0
                    curs[n] = s.cur
                fn(sub, kc)
                for n, s in self.sems.items():
                    s.cur = curs[n]
            for n, s in self.sems.items():
                self.off[n] = self.off.get(n, 0) + trip * pers[n]
            return
        with Forw(self.nc, 0, trip) as kc:
            sub = Ctx(self.nc, self.sems, counting=False)
            curs = {}
            for n, s in self.sems.items():
                o = self.off.get(n, 0)
                sub.off[n] = o + kc * pers[n] if pers[n] else o
                curs[n] = s.cur
            fn(sub, kc)
            for n, s in self.sems.items():
                s.cur = curs[n]
        for n, s in self.sems.items():
            if pers[n]:
                self.off[n] = self.off.get(n, 0) + trip * pers[n]


class SemSet:
    def __init__(self, nc, stack, names):
        self.nc = nc
        self.sems = {}
        for n in names:
            self.sems[n] = _Sem(stack.enter_context(nc.semaphore(n)))

    def _count(self, body, inner_reps):
        cnt = Ctx(self.nc, self.sems, counting=True, inner_reps=inner_reps)
        saved = {n: s.cur for n, s in self.sems.items()}
        for s in self.sems.values():
            s.cur = 0
        body(cnt, 0)
        pers = {n: s.cur for n, s in self.sems.items()}
        ipl = cnt.inner_pers_list
        for n, s in self.sems.items():
            s.cur = saved[n]
        return pers, ipl, cnt.marks_out

    def run_loop(self, nc, g0, gend, body, tri=False, inner_cums=None):
        if gend <= g0:
            return
        persA, iplA, marksA = self._count(body, 0)
        if tri or inner_cums is not None:
            persB, iplB, marksB = self._count(body, 1)
            inner_pers = iplB
            base_pers = persA
        else:
            inner_pers = iplA
            base_pers = persA
        with nc.Fori(g0, gend) as gv:
            ctx = Ctx(nc, self.sems, counting=False)
            ctx.inner_pers_list = inner_pers
            ctx._per = dict(base_pers)
            ctx.premarks = marksA
            grel = gv - g0 if g0 else gv
            if tri:
                trisum = (gv * (gv - 1)) // 2 - (g0 * (g0 - 1)) // 2
            for n, s in self.sems.items():
                o = 0
                if base_pers[n]:
                    o = grel * base_pers[n]
                if tri:
                    stot = sum(pc[n] for pc in inner_pers)
                    if stot:
                        o = o + trisum * stot
                elif inner_cums is not None:
                    for (cum_fn, _tot), pc in zip(inner_cums, inner_pers):
                        if pc[n]:
                            o = o + cum_fn(gv) * pc[n]
                ctx.off[n] = o
            body(ctx, gv)
        ninter = gend - g0
        tot_tri = sum(range(g0, gend))
        for n, s in self.sems.items():
            s.pre += ninter * base_pers[n]
            if tri:
                stot = sum(pc[n] for pc in inner_pers)
                s.pre += tot_tri * stot
            elif inner_cums is not None:
                for (_cf, tot), pc in zip(inner_cums, inner_pers):
                    s.pre += tot * pc[n]
            s.cur = 0

    def straight(self, nc):
        return Ctx(nc, self.sems, counting=False)

    def flush_straight(self):
        for s in self.sems.values():
            s.pre += s.cur
            s.cur = 0


def build_nc(T=4096, do_ffn1=True, do_attn=True, do_ffn2=True):
    NT = T // P
    NG = T // 512
    nc = bacc_mod.Bacc("TRN2")

    x_in = nc.dram_tensor("x", [T, D], F32, kind="ExternalInput").ap()
    wg_in = nc.dram_tensor("wg", [I_FFN, D], F32, kind="ExternalInput").ap()
    wu_in = nc.dram_tensor("wu", [I_FFN, D], F32, kind="ExternalInput").ap()
    wd_in = nc.dram_tensor("wd", [D, I_FFN], F32, kind="ExternalInput").ap()
    wqkvT_in = nc.dram_tensor("wqkvT", [D, 3 * D], F32, kind="ExternalInput").ap()
    wpT_in = nc.dram_tensor("wpT", [D, D], F32, kind="ExternalInput").ap()
    out = nc.dram_tensor("out", [T, D], F32, kind="ExternalOutput").ap()

    qwg = nc.dram_tensor("qwg", [D, I_FFN], BF16).ap()
    qwu = nc.dram_tensor("qwu", [D, I_FFN], BF16).ap()
    qwd = nc.dram_tensor("qwd", [I_FFN, D], BF16).ap()
    Qs = nc.dram_tensor("Qs", [D, T], BF16).ap()
    Ks = nc.dram_tensor("Ks", [D, T], BF16).ap()
    Vs = nc.dram_tensor("Vs", [T, VW], BF16).ap()
    wqkvb = nc.dram_tensor("wqkvb", [D, 3 * D], BF16).ap()
    escr = nc.dram_tensor("escr", [4 * 512], F32).ap()
    abscr = nc.dram_tensor("abscr", [2, 4 * 512], BF16).ap()
    vascr = nc.dram_tensor("vascr", [VW], F32).ap()

    with ExitStack() as stack:
        def psum(name, shape, dt=F32):
            return stack.enter_context(nc.psum_tensor(name, shape, dt))

        def sbuf(name, shape, dt=F32):
            return stack.enter_context(nc.sbuf_tensor(name, shape, dt))

        stream = sbuf("stream", [P, NT * D])
        ident_b = sbuf("ident_b", [P, P], BF16)
        ident_f = sbuf("ident_f", [P, P])
        trimask = sbuf("trimask", [P, P])
        trimaskT = sbuf("trimaskT", [P, P])
        epscol = sbuf("epscol", [P, 1])
        ones_row = sbuf("ones_row", [1, 512], BF16)
        zero_row = sbuf("zero_row", [1, P], BF16)
        iot = stack.enter_context(nc.sbuf_tensor("iot", [P, P], I32))

        psALL = psum("psALL", [P, 8 * 512])

        def bank(i, w=512):
            return psALL[:, i * 512:i * 512 + w]

        def bankb(i):
            return psALL[:].bitcast(BF16)[:, i * 1024:(i + 1) * 1024]

        SS = SemSet(nc, stack, ["dx", "dw", "dw2", "dqw", "dk", "dq", "dq2",
                                "dkv", "pe", "vv", "ss", "gg", "st"])

        def arena_fence():
            cfz = SS.straight(nc)
            for eng in (cfz.V, cfz.S, cfz.G, cfz.TE, cfz.SY):
                for sm in ("dx", "dw", "dw2", "dqw", "dk", "dq", "dq2",
                           "dkv", "st"):
                    cfz.wait(eng, sm)
                for sm in ("vv", "ss", "gg", "pe"):
                    cfz.wait(eng, sm)
            SS.flush_straight()

        c0 = SS.straight(nc)
        c0.G.iota(out=iot[:], pattern=[[1, P]], base=0,
                  channel_multiplier=-1).then_inc("gg", 1)
        c0.G.memset(epscol[:], 1e-5).then_inc("gg", 1)
        c0.G.memset(ones_row[:], 1.0).then_inc("gg", 1)
        c0.G.memset(zero_row[:], 0.0).then_inc("gg", 1)
        c0.G.dma_start(out=wqkvb, in_=wqkvT_in).then_inc("dw2", 16)
        c0.SY.dma_start(out=stream[:, 0:NT * D].rearrange("p (t d) -> p t d", d=D),
                        in_=x_in.rearrange("(t p) d -> p t d", p=P)
                        ).then_inc("dx", 16)
        c0.wait(c0.V, "gg")
        c0.V.tensor_scalar(out=ident_b[:], in0=iot[:], scalar1=0, scalar2=1.0,
                           op0=ALU.is_equal, op1=ALU.mult).then_inc("vv", 1)
        c0.V.tensor_scalar(out=ident_f[:], in0=iot[:], scalar1=0, scalar2=1.0,
                           op0=ALU.is_equal, op1=ALU.mult).then_inc("vv", 1)
        c0.V.tensor_scalar(out=trimask[:], in0=iot[:], scalar1=0, scalar2=NEG,
                           op0=ALU.is_gt, op1=ALU.mult).then_inc("vv", 1)
        c0.V.tensor_scalar(out=trimaskT[:], in0=iot[:], scalar1=0, scalar2=NEG,
                           op0=ALU.is_lt, op1=ALU.mult).then_inc("vv", 1)
        SS.flush_straight()

        def make_quant_bufs(ar, tA=None, pairs=False, alt=None):
            B = {}
            if tA is None:
                B["tA"] = [ar.sbuf_tensor(f"q_tA{i}", [P, 512], F32) for i in range(2)]
            else:
                B["tA"] = tA
            if pairs:
                B["tB"] = [ar.sbuf_tensor(f"q_tB{i}", [P, 512], F32) for i in range(2)]
                B["sgn"] = [ar.sbuf_tensor(f"q_sgn{i}", [P, 512], BF16) for i in range(2)]
            elif alt is not None:
                B["tB"] = [ar.sbuf_tensor("q_tB0", [P, 512], F32), alt[0]]
                B["sgn"] = [ar.sbuf_tensor("q_sgn0", [P, 512], BF16), alt[1]]
            else:
                B["tB"] = [ar.sbuf_tensor("q_tB0", [P, 512], F32)] * 2
                B["sgn"] = [ar.sbuf_tensor("q_sgn0", [P, 512], BF16)] * 2
            B["tC"] = ar.sbuf_tensor("q_tC", [P, 512], F32)
            B["Lb"] = ar.sbuf_tensor("q_Lb", [P, 512], BF16)
            B["gsb"] = ar.sbuf_tensor("q_gsb", [P, 512], BF16)
            B["g1b"] = B["gsb"]
            B["Mblk"] = ar.sbuf_tensor("q_Mblk", [P, 32], F32)
            B["s6"] = [ar.sbuf_tensor(f"q_s6{i}", [P, 32], F32) for i in range(2)]
            B["r6"] = B["Mblk"]
            return B

        def make_ln_bufs(ar, jnk=None):
            B = {}
            B["lncol"] = ar.sbuf_tensor("ln_col", [P, 8], F32)
            B["xlnt"] = ar.sbuf_tensor("ln_x", [P, D], F32)
            if jnk is None:
                B["jnk1"] = B["xlnt"][:, 0:512]
                B["jnk2"] = B["xlnt"][:, 512:1024]
            else:
                B["jnk1"], B["jnk2"] = jnk
            return B

        def equant_v1(ctx, B, src_ap, w, sl=0):
            nblk = w // 16
            src3 = src_ap.rearrange("p (g b) -> p g b", b=16)
            ctx.wait(ctx.V, "vv")
            ctx.V.tensor_reduce(out=B["Mblk"][:, 0:nblk], in_=src3, axis=AX.X,
                                op=ALU.max,
                                apply_absolute_value=True).then_inc("vv", 1)
            ctx.wait(ctx.V, "vv")
            ctx.V.tensor_scalar(out=B["s6"][sl][:, 0:nblk], in0=B["Mblk"][:, 0:nblk],
                                scalar1=6e-12, scalar2=1.0 / 6.0,
                                op0=ALU.max, op1=ALU.mult).then_inc("vv", 1)
            ctx.wait(ctx.V, "vv")
            ctx.V.reciprocal(out=B["r6"][:, 0:nblk],
                             in_=B["s6"][sl][:, 0:nblk]).then_inc("vv", 1)
            ctx.wait(ctx.V, "vv")
            r4 = B["r6"][:, 0:nblk].unsqueeze(2).unsqueeze(3).broadcast_to([P, nblk, 16, 1])
            ctx.V.tensor_tensor(
                out=B["tA"][sl][:, 0:w].rearrange("p (g b) -> p g b", b=16).unsqueeze(3),
                in0=src3.unsqueeze(3), in1=r4, op=ALU.mult).then_inc("vv", 1)

        def equant_s(ctx, B, w, sl=0, skip_wait=False, slb=None):
            slb = sl if slb is None else slb
            if not skip_wait:
                ctx.wait(ctx.S, "vv")
            ctx.S.activation(out=B["tB"][slb][:, 0:w], in_=B["tA"][sl][:, 0:w],
                             func=AF.Abs).then_inc("ss", 1)
            ctx.S.activation(out=B["sgn"][slb][:, 0:w], in_=B["tA"][sl][:, 0:w],
                             func=AF.Sign).then_inc("ss", 1)

        def equant_v2(ctx, B, dst_ap, w, sl=0, skip_wait=False, slb=None):
            nblk = w // 16
            slb = sl if slb is None else slb
            tB, tC = B["tB"][slb], B["tC"]
            Lb = B["Lb"]
            g1b, gsb = B["g1b"], B["gsb"]
            if not skip_wait:
                ctx.wait(ctx.V, "ss")
            ctx.V.tensor_scalar(out=tC[:, 0:w], in0=tB[:, 0:w], scalar1=4.0,
                                scalar2=C3, op0=ALU.max,
                                op1=ALU.add).then_inc("vv", 1)
            ctx.wait(ctx.V, "vv")
            ctx.V.tensor_scalar(out=gsb[:, 0:w], in0=tC[:, 0:w], scalar1=C3,
                                scalar2=6.0, op0=ALU.subtract,
                                op1=ALU.subtract).then_inc("vv", 1)
            ctx.wait(ctx.V, "vv")
            ctx.V.tensor_scalar(out=tC[:, 0:w], in0=tB[:, 0:w], scalar1=2.0,
                                scalar2=C1, op0=ALU.min,
                                op1=ALU.add).then_inc("vv", 1)
            ctx.wait(ctx.V, "vv")
            ctx.V.scalar_tensor_tensor(out=g1b[:, 0:w], in0=tC[:, 0:w],
                                       scalar=C1, in1=g1b[:, 0:w],
                                       op0=ALU.subtract,
                                       op1=ALU.add).then_inc("vv", 1)
            ctx.wait(ctx.V, "vv")
            ctx.V.tensor_scalar(out=tC[:, 0:w], in0=tB[:, 0:w], scalar1=2.0,
                                scalar2=4.0, op0=ALU.max,
                                op1=ALU.min).then_inc("vv", 1)
            ctx.wait(ctx.V, "vv")
            ctx.V.tensor_scalar(out=tC[:, 0:w], in0=tC[:, 0:w], scalar1=C2,
                                scalar2=None, op0=ALU.add).then_inc("vv", 1)
            ctx.wait(ctx.V, "vv")
            ctx.V.scalar_tensor_tensor(out=g1b[:, 0:w], in0=tC[:, 0:w],
                                       scalar=C2, in1=g1b[:, 0:w],
                                       op0=ALU.subtract,
                                       op1=ALU.add).then_inc("vv", 1)
            ctx.wait(ctx.V, "vv")
            ctx.V.tensor_tensor(out=gsb[:, 0:w], in0=gsb[:, 0:w],
                                in1=B["sgn"][slb][:, 0:w],
                                op=ALU.mult).then_inc("vv", 1)
            ctx.wait(ctx.V, "vv")
            s4 = B["s6"][sl][:, 0:nblk].unsqueeze(2).unsqueeze(3).broadcast_to([P, nblk, 16, 1])
            ctx.V.tensor_tensor(
                out=dst_ap.rearrange("p (g b) -> p g b", b=16).unsqueeze(3),
                in0=gsb[:, 0:w].rearrange("p (g b) -> p g b", b=16).unsqueeze(3),
                in1=s4, op=ALU.mult).then_inc("vv", 1)

        def emit_quant(ctx, B, src_ap, dst_ap, w, sl=0):
            equant_v1(ctx, B, src_ap, w, sl)
            equant_s(ctx, B, w, sl)
            equant_v2(ctx, B, dst_ap, w, sl)

        def emit_ln(ctx, B, xsl):
            lncol, xlnt = B["lncol"], B["xlnt"]
            jnk1, jnk2 = B["jnk1"], B["jnk2"]
            ctx.wait(ctx.S, "vv")
            ctx.wait(ctx.S, "gg")
            ctx.S.activation(out=jnk1, in_=xsl[:, 0:512], func=AF.Square,
                             accum_out=lncol[:, 0:1]).then_inc("ss", 1)
            ctx.S.activation(out=jnk2, in_=xsl[:, 512:1024], func=AF.Square,
                             accum_out=lncol[:, 7:8]).then_inc("ss", 1)
            ctx.V.tensor_reduce(out=lncol[:, 1:2], in_=xsl, axis=AX.X,
                                op=ALU.add).then_inc("vv", 1)
            ctx.wait(ctx.V, "ss")
            ctx.V.tensor_tensor(out=lncol[:, 0:1], in0=lncol[:, 0:1],
                                in1=lncol[:, 7:8], op=ALU.add).then_inc("vv", 1)
            ctx.wait(ctx.V, "vv")
            ctx.V.tensor_scalar(out=lncol[:, 2:3], in0=lncol[:, 1:2],
                                scalar1=1.0 / D, scalar2=None,
                                op0=ALU.mult).then_inc("vv", 1)
            ctx.wait(ctx.V, "vv")
            ctx.V.tensor_tensor(out=lncol[:, 3:4], in0=lncol[:, 2:3],
                                in1=lncol[:, 2:3], op=ALU.mult).then_inc("vv", 1)
            ctx.wait(ctx.V, "ss")
            ctx.wait(ctx.V, "vv")
            ctx.V.scalar_tensor_tensor(out=lncol[:, 4:5], in0=lncol[:, 0:1],
                                       scalar=1.0 / D, in1=lncol[:, 3:4],
                                       op0=ALU.mult,
                                       op1=ALU.subtract).then_inc("vv", 1)
            ctx.wait(ctx.S, "vv")
            ctx.S.activation(out=lncol[:, 5:6], in_=lncol[:, 4:5],
                             func=AF.Sqrt, bias=epscol[:, 0:1],
                             scale=1.0).then_inc("ss", 1)
            ctx.wait(ctx.V, "ss")
            ctx.V.reciprocal(out=lncol[:, 6:7],
                             in_=lncol[:, 5:6]).then_inc("vv", 1)
            ctx.wait(ctx.V, "vv")
            ctx.V.tensor_scalar(out=xlnt[:], in0=xsl, scalar1=lncol[:, 2:3],
                                scalar2=lncol[:, 6:7], op0=ALU.subtract,
                                op1=ALU.mult).then_inc("vv", 1)

        wq_ar = nc.auto_arena()
        QB_wq = make_quant_bufs(wq_ar, pairs=True)
        wqin = [wq_ar.sbuf_tensor(f"wqin{i}", [P, I_FFN], F32) for i in range(2)]
        wqout = [wq_ar.sbuf_tensor(f"wqout{i}", [P, I_FFN], BF16) for i in range(2)]
        wqTev = [wq_ar.sbuf_tensor(f"wqTev{i}", [P, 22 * P], BF16) for i in range(2)]
        wq_ar.allocate()

        def make_wq_body(src_dram, dst_dram, width, ntiles):
            ntile_t = width // P
            ncw = (width + 511) // 512

            def emit_tp_store(ctx, slot, r):
                ctx.wait(ctx.TE, "vv")
                ctx.wait(ctx.S, "dqw")
                for j in range(ntile_t):
                    gj = j % 2
                    if gj == 0:
                        ctx.wait(ctx.TE, "ss")
                    w_ = ctx.TE.transpose(
                        out=bankb(6)[:, gj * P:(gj + 1) * P],
                        in_=wqout[slot][:, ts(j, P)], identity=ident_b[:])
                    if gj == 1 or j == ntile_t - 1:
                        w_.then_inc("pe", 1)
                        ctx.wait(ctx.S, "pe")
                        ctx.S.activation(
                            out=wqTev[slot][:, (j - gj) * P:(j + 1) * P],
                            in_=bankb(6)[:, 0:(gj + 1) * P],
                            func=AF.Copy, bias=0.0,
                            scale=1.0).then_inc("ss", 1)
                ctx.wait(ctx.SY, "ss")
                ctx.SY.dma_start(
                    out=dst_dram.rearrange("(dc p) c -> p dc c", p=P)
                    [:, :, ds(r * P, P)],
                    in_=wqTev[slot][:, 0:ntile_t * P].rearrange(
                        "p (dc c) -> p dc c", c=P)).then_inc("dqw", 16)

            def body(ctx, r2):
                items = [(s, c) for s in range(2) for c in range(ncw)]
                ctx.wait(ctx.V, "dw")
                ctx.wait(ctx.V, "pe")
                for idx, (s, c) in enumerate(items):
                    w512 = min(512, width - c * 512)
                    sl = idx % 2
                    equant_v1(ctx, QB_wq, wqin[s][:, c * 512:c * 512 + w512],
                              w512, sl=sl)
                    equant_s(ctx, QB_wq, w512, sl=sl)
                    if idx > 0:
                        sp, cp = items[idx - 1]
                        wp = min(512, width - cp * 512)
                        ctx.wait(ctx.V, "ss", back=2)
                        equant_v2(ctx, QB_wq,
                                  wqout[sp][:, cp * 512:cp * 512 + wp], wp,
                                  sl=(idx - 1) % 2, skip_wait=True)
                        if cp == ncw - 1:
                            emit_tp_store(ctx, sp, 2 * r2 + sp)
                    if c == ncw - 1:
                        ctx.wait(ctx.SY, "vv")
                        ctx.wait(ctx.SY, "dw")
                        ctx.SY.dma_start(
                            out=wqin[s][:, 0:width],
                            in_=src_dram.rearrange("(t p) d -> t p d", p=P)
                            [(2 * r2 + s + 2) % ntiles]).then_inc("dw", 16)
                sp, cp = items[-1]
                wp = min(512, width - cp * 512)
                ctx.wait(ctx.V, "ss")
                equant_v2(ctx, QB_wq, wqout[sp][:, cp * 512:cp * 512 + wp],
                          wp, sl=(len(items) - 1) % 2, skip_wait=True)
                emit_tp_store(ctx, sp, 2 * r2 + sp)
            return body

        def wq_prologue(src_dram, width):
            cpro = SS.straight(nc)
            for s in range(2):
                cpro.wait(cpro.SY, "dw")
                cpro.SY.dma_start(
                    out=wqin[s][:, 0:width],
                    in_=src_dram.rearrange("(t p) d -> t p d", p=P)[s]
                    ).then_inc("dw", 16)
            SS.flush_straight()

        wq_prologue(wg_in, D)
        SS.run_loop(nc, 0, I_FFN // P // 2, make_wq_body(wg_in, qwg, D, I_FFN // P))
        wq_prologue(wu_in, D)
        SS.run_loop(nc, 0, I_FFN // P // 2, make_wq_body(wu_in, qwu, D, I_FFN // P))
        wq_prologue(wd_in, I_FFN)
        SS.run_loop(nc, 0, D // P // 2, make_wq_body(wd_in, qwd, I_FFN, D // P))
        arena_fence()
        wq_ar.free()

        def emit_ffn_phase():
            ar = nc.auto_arena()
            LB = make_ln_bufs(ar, jnk=None)
            QB = make_quant_bufs(ar, alt=(
                LB["xlnt"][:, 0:512],
                LB["xlnt"][:].bitcast(BF16)[:, 1024:1536]))
            xqT = ar.sbuf_tensor("xqT", [P, 32 * P], BF16)
            wgb = [ar.sbuf_tensor(f"wgb{i}", [P, 8 * 512], BF16) for i in range(2)]
            wub = [ar.sbuf_tensor(f"wub{i}", [P, 8 * 512], BF16) for i in range(2)]
            wdb = [ar.sbuf_tensor(f"wdb{i}", [P, 4 * D], BF16) for i in range(2)]
            sg_sb = ar.sbuf_tensor("sg_sb", [P, 512], F32)
            xq = sg_sb[:].bitcast(BF16)
            hbuf = psALL[:, 7 * 512:8 * 512]
            hq = ar.sbuf_tensor("hq", [P, 512], BF16)
            hqT = ar.sbuf_tensor("hqT", [P, 4 * P], BF16)
            ar.allocate()

            def ffn_body(ctx, g):
                NCK = 6
                xqT4 = xqT[:].rearrange("p (dm t c) -> p dm t c", t=4, c=P)
                ctx.wait(ctx.V, "dx")
                ctx.wait(ctx.S, "dx")
                xsl0 = stream[:, ds(g * (4 * D) + 0 * D, D)]
                emit_ln(ctx, LB, xsl0)
                for ti in range(4):
                    for hhf in range(2):
                        equant_v1(ctx, QB, LB["xlnt"][:, hhf * 512:(hhf + 1) * 512],
                                  512, sl=hhf)
                    equant_s(ctx, QB, 512, sl=0, slb=0)
                    if ti < 3:
                        xsl = stream[:, ds(g * (4 * D) + (ti + 1) * D, D)]
                        emit_ln(ctx, LB, xsl)
                    ctx.wait(ctx.V, "pe")
                    equant_v2(ctx, QB, xq[:, 0:512], 512, sl=0, slb=0)
                    equant_s(ctx, QB, 512, sl=1, slb=0)
                    equant_v2(ctx, QB, xq[:, 512:1024], 512, sl=1, slb=0)
                    ctx.wait(ctx.TE, "vv")
                    for dm in range(8):
                        gj = dm % 2
                        if gj == 0:
                            ctx.wait(ctx.TE, "ss")
                        w_ = ctx.TE.transpose(
                            out=bankb(6)[:, gj * P:(gj + 1) * P],
                            in_=xq[:, ts(dm, P)], identity=ident_b[:])
                        if gj == 1:
                            w_.then_inc("pe", 1)
                            ctx.wait(ctx.S, "pe")
                            ctx.S.activation(
                                out=xqT4[:, dm - 1:dm + 1, ti, :],
                                in_=bankb(6)[:, 0:2 * P].rearrange(
                                    "p (g c) -> p g c", c=P),
                                func=AF.Copy, bias=0.0,
                                scale=1.0).then_inc("ss", 1)
                    ctx.set_mark(("xev", ti), "ss")

                NIT = NCK * 4
                wgv = [wgb[i][:].rearrange("p (dc i) -> p dc i", i=512) for i in range(2)]
                wuv = [wub[i][:].rearrange("p (dc i) -> p dc i", i=512) for i in range(2)]
                wdv = [wdb[i][:].rearrange("p (ic d) -> p ic d", d=D) for i in range(2)]

                def mk(name, k):
                    return (name, k % NIT)

                def lagof(k):
                    return 1 if k < 0 else 0

                def wm(eng, name, k):
                    ctx.wait_mark(eng, ctx.get_mark(mk(name, k)), lag=lagof(k))

                def wload(c, lag=0):
                    par = c % 2
                    Ic = 512 if c < 5 else 256
                    k_free = (c - 2) * 4 + 3
                    wm(ctx.SY, "st0", k_free)
                    ctx.wait(ctx.SY, "dqw")
                    ctx.wait(ctx.SY, "dw")
                    ctx.SY.dma_start(
                        out=wgv[par][:, :, 0:Ic],
                        in_=qwg.rearrange("(dc p) i -> p dc i", p=P)[:, :, c * 512:c * 512 + Ic]
                        ).then_inc("dw", 16)
                    ctx.set_mark(("wbg", c), "dw")
                    ctx.wait(ctx.SY, "dw")
                    ctx.SY.dma_start(
                        out=wuv[par][:, :, 0:Ic],
                        in_=qwu.rearrange("(dc p) i -> p dc i", p=P)[:, :, c * 512:c * 512 + Ic]
                        ).then_inc("dw", 16)
                    ctx.set_mark(("wbu", c), "dw")

                def wdload(c):
                    par = c % 2
                    Ic = 512 if c < 5 else 256
                    nsub = Ic // P
                    k_free = (c - 2) * 4 + 3
                    wm(ctx.G, "st9", k_free)
                    ctx.wait(ctx.G, "dqw")
                    ctx.wait(ctx.G, "dw2")
                    ctx.G.dma_start(
                        out=wdv[par][:, 0:nsub, :],
                        in_=qwd.rearrange("(ic p) d -> p ic d", p=P)[:, 4 * c:4 * c + nsub, :]
                        ).then_inc("dw2", 16)

                def sizes(k):
                    c, ti = k // 4, k % 4
                    Ic = 512 if c < 5 else 256
                    return c, ti, Ic, Ic // P

                def st0(k):
                    c, ti, Ic, nsub = sizes(k)
                    ctx.wait_mark(ctx.TE, ctx.get_mark(("wbg", c)), lag=0)
                    if k < 2:
                        ctx.wait_mark(ctx.TE, ctx.get_mark(("xev", ti)), lag=0)
                    wm(ctx.TE, "silu", k - 2)
                    for dm in range(8):
                        w_ = ctx.TE.matmul(out=bank(k % 2, Ic),
                                           lhsT=xqT4[:, dm, ti, :],
                                           rhs=wgv[c % 2][:, dm, 0:Ic],
                                           start=(dm == 0), stop=(dm == 7))
                    ctx.wait_mark(ctx.TE, ctx.get_mark(("wbu", c)), lag=0)
                    wm(ctx.TE, "hm", k - 2)
                    for dm in range(8):
                        w_ = ctx.TE.matmul(out=bank(2 + k % 2, Ic),
                                           lhsT=xqT4[:, dm, ti, :],
                                           rhs=wuv[c % 2][:, dm, 0:Ic],
                                           start=(dm == 0), stop=(dm == 7))
                    w_.then_inc("pe", 1)
                    ctx.set_mark(mk("st0", k), "pe")

                def st1(k):
                    c, ti, Ic, nsub = sizes(k)
                    wm(ctx.S, "st0", k)
                    wm(ctx.S, "hm", k - 1)
                    ctx.S.activation(out=sg_sb[:, 0:Ic], in_=bank(k % 2, Ic),
                                     func=AF.Silu).then_inc("ss", 1)
                    ctx.set_mark(mk("silu", k), "ss")

                def st23(k):
                    c, ti, Ic, nsub = sizes(k)
                    wm(ctx.V, "silu", k)
                    ctx.V.tensor_tensor(out=hbuf[:, 0:Ic], in0=sg_sb[:, 0:Ic],
                                        in1=bank(2 + k % 2, Ic),
                                        op=ALU.mult).then_inc("vv", 1)
                    ctx.set_mark(mk("hm", k), "vv")
                    equant_v1(ctx, QB, hbuf[:, 0:Ic], Ic, sl=k % 2)
                    ctx.set_mark(mk("v1", k), "vv")

                def st4(k):
                    c, ti, Ic, nsub = sizes(k)
                    wm(ctx.S, "v1", k)
                    wm(ctx.S, "q", k - 2)
                    equant_s(ctx, QB, Ic, sl=k % 2, skip_wait=True)
                    ctx.set_mark(mk("sgn", k), "ss")

                def st6(k):
                    c, ti, Ic, nsub = sizes(k)
                    wm(ctx.V, "sgn", k)
                    wm(ctx.V, "st7", k - 1)
                    equant_v2(ctx, QB, hq[:, 0:Ic], Ic, sl=k % 2, skip_wait=True)
                    ctx.set_mark(mk("q", k), "vv")

                def st7(k):
                    c, ti, Ic, nsub = sizes(k)
                    wm(ctx.TE, "q", k)
                    wm(ctx.TE, "st8", k - 1)
                    w_ = None
                    for j in range(nsub):
                        w_ = ctx.TE.transpose(
                            out=bankb(6)[:, j * P:(j + 1) * P],
                            in_=hq[:, ts(j, P)], identity=ident_b[:])
                    w_.then_inc("pe", 1)
                    ctx.set_mark(mk("st7", k), "pe")

                def st8(k):
                    c, ti, Ic, nsub = sizes(k)
                    wm(ctx.S, "st7", k)
                    wm(ctx.S, "st9", k - 1)
                    ctx.S.activation(out=hqT[:, 0:nsub * P],
                                     in_=bankb(6)[:, 0:nsub * P],
                                     func=AF.Copy, bias=0.0,
                                     scale=1.0).then_inc("ss", 1)
                    ctx.set_mark(mk("st8", k), "ss")

                def st9(k):
                    c, ti, Ic, nsub = sizes(k)
                    ctx.wait_mark(ctx.TE,
                                  ctx.get_mark(("wbd", 0 if c < 2 else c)), lag=0)
                    wm(ctx.TE, "st8", k)
                    wm(ctx.TE, "st10", k - 1)
                    for oh in range(2):
                        for isub in range(nsub):
                            w_ = ctx.TE.matmul(out=bank(4 + oh),
                                               lhsT=hqT[:, ts(isub, P)],
                                               rhs=wdv[c % 2][:, isub, oh * 512:(oh + 1) * 512],
                                               start=(isub == 0),
                                               stop=(isub == nsub - 1))
                        w_.then_inc("pe", 1)
                    ctx.set_mark(mk("st9", k), "pe")

                def st10(k):
                    c, ti, Ic, nsub = sizes(k)
                    wm(ctx.V, "st9", k)
                    ctx.V.scalar_tensor_tensor(
                        out=stream[:, ds(g * (4 * D) + ti * D, 1024)],
                        in0=psALL[:, 4 * 512:6 * 512], scalar=0.5,
                        in1=stream[:, ds(g * (4 * D) + ti * D, 1024)],
                        op0=ALU.mult, op1=ALU.add).then_inc("vv", 1)
                    ctx.set_mark(mk("st10", k), "vv")

                wload(0)
                wload(1)
                wdload(0)
                wdload(1)
                ctx.set_mark(("wbd", 0), "dw2")
                for b in range(NIT + 5):
                    if b < NIT:
                        st0(b)
                    if 0 <= b - 3 < NIT:
                        st7(b - 3)
                    if 0 <= b - 4 < NIT:
                        st9(b - 4)
                    if 0 <= b - 3 < NIT:
                        st8(b - 3)
                    if 0 <= b - 1 < NIT:
                        st1(b - 1)
                        st4(b - 1)
                    if 0 <= b - 2 < NIT:
                        st6(b - 2)
                    if 0 <= b - 1 < NIT:
                        st23(b - 1)
                    if 0 <= b - 4 < NIT:
                        st10(b - 4)
                    if b % 4 == 0:
                        if 2 <= b // 4 + 1 <= NCK - 1:
                            wload(b // 4 + 1)
                        if 2 <= b // 4 + 2 <= NCK - 1:
                            wdload(b // 4 + 2)
                            ctx.set_mark(("wbd", b // 4 + 2), "dw2")

            SS.run_loop(nc, 0, NG, ffn_body)
            arena_fence()
            ar.free()

        if do_ffn1:
            emit_ffn_phase()

        if do_attn:
            qkv_ar = nc.auto_arena()
            LBq = make_ln_bufs(qkv_ar)
            xlnT = qkv_ar.sbuf_tensor("xlnT", [P, 32 * P], BF16)
            wkb = [qkv_ar.sbuf_tensor(f"wkb{i}", [P, 8 * 512], BF16) for i in range(2)]
            Vev = qkv_ar.sbuf_tensor("Vev", [P, 4 * VW], BF16)
            qkev = qkv_ar.sbuf_tensor("qkev", [P, 2 * 512], BF16)
            qkv_ar.allocate()

            cq = SS.straight(nc)
            cq.wait(cq.G, "dx")
            cq.G.memset(Vev[:].rearrange("p (t h c) -> p t h c", h=NH, c=HD + 1)
                        [:, :, :, HD:HD + 1], 1.0).then_inc("gg", 1)
            SS.flush_straight()

            def qkv_body(ctx, g):
                xlnT4 = xlnT[:].rearrange("p (dm t c) -> p dm t c", t=4, c=P)
                ctx.wait(ctx.V, "dx")
                ctx.wait(ctx.S, "dx")
                for ti in range(4):
                    xsl = stream[:, ds(g * (4 * D) + ti * D, D)]
                    emit_ln(ctx, LBq, xsl)
                    ctx.wait(ctx.TE, "vv")
                    for dm in range(8):
                        gj = dm % 2
                        if gj == 0:
                            ctx.wait(ctx.TE, "ss")
                        w_ = ctx.TE.transpose(
                            out=bank(6)[:, gj * P:(gj + 1) * P],
                            in_=LBq["xlnt"][:, ts(dm, P)], identity=ident_f[:])
                        if gj == 1:
                            w_.then_inc("pe", 1)
                            ctx.wait(ctx.S, "pe")
                            ctx.S.activation(
                                out=xlnT4[:, dm - 1:dm + 1, ti, :],
                                in_=bank(6)[:, 0:2 * P].rearrange(
                                    "p (g c) -> p g c", c=P),
                                func=AF.Copy, bias=0.0,
                                scale=1.0).then_inc("ss", 1)
                for ck in range(6):
                    par = ck % 2
                    ctx.wait(ctx.SY, "pe")
                    ctx.wait(ctx.SY, "dw2")
                    ctx.wait(ctx.SY, "dk")
                    ctx.SY.dma_start(
                        out=wkb[par][:].rearrange("p (dc o) -> p dc o", o=512),
                        in_=wqkvb.rearrange("(dc p) o -> p dc o", p=P)
                        [:, :, ck * 512:(ck + 1) * 512]).then_inc("dk", 16)
                    if ck < 4:
                        for oj in range(4):
                            bk = oj % 2
                            ctx.wait(ctx.TE, "dk")
                            ctx.wait(ctx.TE, "ss")
                            for dc in range(8):
                                w_ = ctx.TE.matmul(
                                    out=bank(bk),
                                    lhsT=wkb[par][:].rearrange(
                                        "p (dc o) -> p dc o", o=512)[:, dc, oj * P:(oj + 1) * P],
                                    rhs=xlnT[:, ts(dc, 512)],
                                    start=(dc == 0), stop=(dc == 7))
                            w_.then_inc("pe", 1)
                            ctx.wait(ctx.S, "pe")
                            dsem = "dq" if oj % 2 == 0 else "dq2"
                            ctx.wait(ctx.S, dsem)
                            qh = qkev[:, (oj % 2) * 512:(oj % 2 + 1) * 512]
                            ctx.S.activation(out=qh, in_=bank(bk),
                                             func=AF.Copy, bias=0.0,
                                             scale=(0.125 if ck < 2 else 1.0)
                                             ).then_inc("ss", 1)
                            ctx.wait(ctx.SY, "ss")
                            dst = Qs if ck < 2 else Ks
                            rr = (ck % 2) * 4 + oj
                            ctx.SY.dma_start(
                                out=dst.rearrange("(r p) t -> p r t", p=P)[:, rr, ds(g * 512, 512)],
                                in_=qh).then_inc(dsem, 16)
                    else:
                        for ti in range(4):
                            bk = ti % 2
                            ctx.wait(ctx.TE, "dk")
                            ctx.wait(ctx.TE, "ss")
                            ctx.wait(ctx.TE, "dq")
                            for dc in range(8):
                                w_ = ctx.TE.matmul(
                                    out=bank(bk),
                                    lhsT=xlnT4[:, dc, ti, :],
                                    rhs=wkb[par][:].rearrange(
                                        "p (dc o) -> p dc o", o=512)[:, dc, :],
                                    start=(dc == 0), stop=(dc == 7))
                            w_.then_inc("pe", 1)
                            ctx.wait(ctx.S, "pe")
                            ctx.wait(ctx.S, "dq")
                            hoff = (ck - 4) * 8 * (HD + 1)
                            ctx.S.activation(
                                out=Vev[:, ti * VW + hoff:ti * VW + hoff + 8 * (HD + 1)]
                                .rearrange("p (h c) -> p h c", c=HD + 1)[:, :, 0:HD],
                                in_=bank(bk).rearrange("p (h c) -> p h c", c=HD),
                                func=AF.Copy, bias=0.0, scale=1.0).then_inc("ss", 1)
                ctx.wait(ctx.SY, "ss")
                ctx.SY.dma_start(
                    out=Vs.rearrange("(b p) c -> p b c", p=P)[:, ds(g * 4, 4), :],
                    in_=Vev[:].rearrange("p (b c) -> p b c", c=VW)
                    ).then_inc("dq", 16)

            SS.run_loop(nc, 0, NG, qkv_body)
            arena_fence()
            qkv_ar.free()

            core_ar = nc.auto_arena()
            qbuf = core_ar.sbuf_tensor("qbuf", [65, 2 * 4 * 512], BF16)
            KbD = core_ar.sbuf_tensor("KbD", [65, 4 * 512], BF16)
            VbDg = core_ar.sbuf_tensor("VbDg", [P, 16 * (HD + 1)], BF16)
            Kb = core_ar.sbuf_tensor("Kb", [65, 2 * 4 * 512], BF16)
            Vb = core_ar.sbuf_tensor("Vb", [P, 2 * 16 * (HD + 1)], BF16)
            eT2 = core_ar.sbuf_tensor("eT2", [P, 2 * 4 * 512], BF16)
            wpTs = core_ar.sbuf_tensor("wpTs", [P, 8 * D], BF16)
            Oqt = core_ar.sbuf_tensor("Oqt", [P, 8 * 512], BF16)
            Ostg = core_ar.sbuf_tensor("Ostg", [64, 512], BF16)
            ABa = core_ar.sbuf_tensor("ABa", [P, 512], F32)
            ABb = core_ar.sbuf_tensor("ABb", [64, 512], F32)
            otmp = core_ar.sbuf_tensor("otmp", [64, 512], F32)
            mcol = core_ar.sbuf_tensor("mcol", [P, 18], F32)
            rowAB = core_ar.sbuf_tensor("rowAB", [1, 2 * 4 * 512], BF16)
            Vall = core_ar.sbuf_tensor("Vall", [HD, NH], F32)
            onecol = core_ar.sbuf_tensor("onecol", [P, 1], BF16)
            core_ar.allocate()
            aparv = otmp[:].bitcast(BF16)[0:4, 0:512]
            bparv = otmp[:].bitcast(BF16)[0:4, 512:1024]
            Eparv = otmp[0:4, :]
            Dparv = ABa[0:4, :]
            Rparv = ABb[0:4, :]
            Vrowv = Oqt[:].bitcast(F32)[0:1, 0:VW]
            Vtilev = Oqt[:, 2080:2080 + VW]

            QsV = Qs.rearrange("(h d) t -> d h t", d=HD)
            KsV = Ks.rearrange("(h d) t -> d h t", d=HD)
            VsV = Vs.rearrange("(kt p) c -> p kt c", p=P)

            cva = SS.straight(nc)
            cva.wait(cva.G, "dq")
            cva.G.memset(onecol[:], 1.0).then_inc("gg", 1)
            cva.wait(cva.TE, "gg")
            w_ = None
            for bk in (5, 6, 7):
                w_ = cva.TE.matmul(out=psALL[0:1, bk * 512:(bk + 1) * 512],
                                   lhsT=zero_row[:, 0:1],
                                   rhs=ones_row[:], start=True, stop=False)
            w_.then_inc("pe", 1)
            SS.flush_straight()

            def vall_body(ctx, t):
                ctx.wait(ctx.SY, "pe")
                ctx.SY.dma_start(out=Vtilev,
                                 in_=Vs.rearrange("(b p) c -> p b c", p=P)[:, t, :]
                                 ).then_inc("dkv", 16)
                ctx.wait(ctx.TE, "dkv")
                ctx.TE.matmul(out=psALL[0:1, 6 * 512:7 * 512], lhsT=onecol[:],
                              rhs=Vtilev[:, 0:512], start=False, stop=False)
                ctx.TE.matmul(out=psALL[0:1, 7 * 512:8 * 512], lhsT=onecol[:],
                              rhs=Vtilev[:, 512:1024], start=False, stop=False)
                w2 = ctx.TE.matmul(out=psALL[0:1, 5 * 512:5 * 512 + 16],
                                   lhsT=onecol[:], rhs=Vtilev[:, 1024:1040],
                                   start=False, stop=False)
                w2.then_inc("pe", 1)

            SS.run_loop(nc, 0, NT, vall_body)
            ccl = SS.straight(nc)
            w_ = None
            for bk in (5, 6, 7):
                w_ = ccl.TE.matmul(out=psALL[0:1, bk * 512:(bk + 1) * 512],
                                   lhsT=zero_row[:, 0:1],
                                   rhs=ones_row[:], start=False, stop=True)
            w_.then_inc("pe", 1)
            SS.flush_straight()

            cvt = SS.straight(nc)
            cvt.wait(cvt.V, "pe")
            cvt.V.tensor_copy(out=Vrowv[0:1, 0:512],
                              in_=psALL[0:1, 6 * 512:7 * 512]).then_inc("vv", 1)
            cvt.V.tensor_copy(out=Vrowv[0:1, 512:1024],
                              in_=psALL[0:1, 7 * 512:8 * 512]).then_inc("vv", 1)
            cvt.V.tensor_copy(out=Vrowv[0:1, 1024:1040],
                              in_=psALL[0:1, 5 * 512:5 * 512 + 16]).then_inc("vv", 1)
            cvt.wait(cvt.SY, "vv")
            cvt.SY.dma_start(out=vascr.rearrange("(o f) -> o f", o=1)[:, 0:512],
                             in_=Vrowv[:, 0:512]).then_inc("dkv", 16)
            cvt.SY.dma_start(out=vascr.rearrange("(o f) -> o f", o=1)[:, 512:1024],
                             in_=Vrowv[:, 512:1024]).then_inc("dkv", 16)
            cvt.SY.dma_start(out=vascr.rearrange("(o f) -> o f", o=1)[:, 1024:1040],
                             in_=Vrowv[:, 1024:1040]).then_inc("dkv", 16)
            cvt.wait(cvt.TE, "vv")
            cvt.wait(cvt.TE, "pe")
            w_ = None
            for h in range(NH):
                w_ = cvt.TE.transpose(out=psALL[0:HD, h:h + 1],
                                      in_=Vrowv[0:1, h * (HD + 1):h * (HD + 1) + HD],
                                      identity=ident_f[0:1, 0:1])
            w_.then_inc("pe", 1)
            cvt.wait(cvt.V, "pe")
            cvt.V.tensor_copy(out=Vall[:, :],
                              in_=psALL[0:HD, 0:NH]).then_inc("vv", 1)
            cvt.G.dma_start(out=wpTs[:].rearrange("p (dc o) -> p dc o", o=D),
                            in_=wpT_in.rearrange("(dc p) o -> p dc o", p=P)
                            ).then_inc("dw2", 16)
            cvt.G.memset(Kb[64:65, :], 1.0).then_inc("gg", 1)
            cvt.G.memset(KbD[64:65, :], 1.0).then_inc("gg", 1)
            cvt.wait(cvt.G, "dq")
            cvt.wait(cvt.G, "dkv")
            cvt.G.dma_start(
                out=qbuf[0:HD, 0:2048].rearrange("d (h t) -> d h t", t=512),
                in_=QsV[:, 0:4, 0:512]).then_inc("dw2", 16)
            cvt.G.dma_start(
                out=KbD[0:HD, :].rearrange("d (h t) -> d h t", t=512),
                in_=KsV[:, 0:4, 0:512]).then_inc("dw2", 16)
            SS.flush_straight()

            def emit_epi_head(ctx, p):
                ctx.wait(ctx.S, "pe")
                for hh in range(4):
                    ctx.wait(ctx.S, "dq")
                    ctx.S.activation(
                        out=ABa[64:65, :],
                        in_=psALL[64:65, (4 + hh) * 512:(5 + hh) * 512],
                        func=AF.Copy, bias=0.0, scale=1.0).then_inc("ss", 1)
                    ctx.wait(ctx.S, "ss")
                    ctx.S.dma_start(
                        out=escr.rearrange("(o f) -> o f", o=1)
                        [0:1, hh * 512:(hh + 1) * 512],
                        in_=ABa[64:65, :]).then_inc("dq", 16)
                ctx.wait(ctx.S, "dq")
                ctx.wait(ctx.S, "vv")
                ctx.S.dma_start(out=Eparv,
                                in_=escr.rearrange("(h f) -> h f", h=4)
                                ).then_inc("dq", 16)
                c1 = 1.0 + float(T) * 1e-8
                c2 = float(T) * 1e-14
                ctx.wait(ctx.V, "dq")
                ctx.V.tensor_scalar(out=Dparv, in0=Eparv, scalar1=c1,
                                    scalar2=c2, op0=ALU.mult,
                                    op1=ALU.add).then_inc("vv", 1)
                ctx.wait(ctx.V, "vv")
                ctx.V.reciprocal(out=Rparv, in_=Dparv).then_inc("vv", 1)
                ctx.wait(ctx.V, "vv")
                ctx.V.tensor_scalar(out=Dparv, in0=Eparv, scalar1=1e-6,
                                    scalar2=None, op0=ALU.add).then_inc("vv", 1)
                ctx.wait(ctx.V, "vv")
                ctx.V.tensor_copy(out=aparv, in_=Rparv).then_inc("vv", 1)
                ctx.wait(ctx.V, "vv")
                ctx.V.scalar_tensor_tensor(out=bparv, in0=Dparv, scalar=1e-8,
                                           in1=Rparv, op0=ALU.mult,
                                           op1=ALU.mult).then_inc("vv", 1)
                ctx.wait(ctx.S, "vv")
                ctx.wait(ctx.S, "pe")
                ctx.S.dma_start(out=abscr[0].rearrange("(h f) -> h f", h=4),
                                in_=aparv).then_inc("dq", 16)
                ctx.S.dma_start(out=abscr[1].rearrange("(h f) -> h f", h=4),
                                in_=bparv).then_inc("dq", 16)
                ctx.wait(ctx.S, "dq")
                ctx.S.dma_start(
                    out=rowAB[:],
                    in_=abscr.rearrange("r f -> (r f)").rearrange(
                        "(o f) -> o f", o=1)).then_inc("dq", 16)

            def emit_epi_tail(ctx, p):
                for hh in range(4):
                    h = 4 * p + hh
                    ctx.wait(ctx.TE, "dq")
                    ctx.wait(ctx.TE, "ss")
                    w_ = ctx.TE.matmul(out=bank(2)[0:HD, :],
                                       lhsT=ones_row[0:1, 0:HD],
                                       rhs=rowAB[0:1, hh * 512:(hh + 1) * 512],
                                       start=True, stop=True)
                    w_ = ctx.TE.matmul(
                        out=bank(3)[0:HD, :],
                        lhsT=ones_row[0:1, 0:HD],
                        rhs=rowAB[0:1, 2048 + hh * 512:2048 + (hh + 1) * 512],
                        start=True, stop=True)
                    w_.then_inc("pe", 1)
                    ctx.wait(ctx.S, "pe")
                    ctx.wait(ctx.S, "vv")
                    ctx.S.activation(out=ABa[0:HD, :], in_=bank(2)[0:HD, :],
                                     func=AF.Copy, bias=0.0,
                                     scale=1.0).then_inc("ss", 1)
                    ctx.S.activation(out=ABb[0:HD, :], in_=bank(3)[0:HD, :],
                                     func=AF.Copy, bias=0.0,
                                     scale=1.0).then_inc("ss", 1)
                    ctx.wait(ctx.V, "ss")
                    ctx.wait(ctx.V, "st")
                    if hh == 0:
                        ctx.wait(ctx.V, "dq")
                    ctx.V.tensor_tensor(out=otmp[:], in0=bank(4 + hh)[0:HD, :],
                                        in1=ABa[0:HD, :],
                                        op=ALU.mult).then_inc("vv", 1)
                    ctx.wait(ctx.V, "vv")
                    ctx.V.scalar_tensor_tensor(
                        out=Ostg[:], in0=ABb[0:HD, :],
                        scalar=Vall[:, h:h + 1], in1=otmp[:],
                        op0=ALU.mult, op1=ALU.add).then_inc("vv", 1)
                    ctx.wait(ctx.SY, "vv")
                    ctx.wait(ctx.SY, "pe")
                    ctx.SY.dma_start(
                        out=Oqt[(h % 2) * HD:(h % 2 + 1) * HD,
                                (h // 2) * 512:(h // 2 + 1) * 512],
                        in_=Ostg[:]).then_inc("st", 16)

            def emit_diag_prep(ctx, p, qt, after_epi):
                qoff = (p % 2) * 2048
                for j in range(18):
                    if j >= 2:
                        jj = j - 2
                        hh, qsub = divmod(jj, 4)
                        wcols = (qsub + 1) * P
                        ctx.wait(ctx.V, "pe")
                        ctx.wait(ctx.V, "vv")
                        ctx.V.tensor_tensor(
                            out=bank(jj % 2)[:, qsub * P:wcols],
                            in0=bank(jj % 2)[:, qsub * P:wcols],
                            in1=trimask[:], op=ALU.add).then_inc("vv", 1)
                        ctx.wait(ctx.V, "vv")
                        ctx.V.tensor_reduce(
                            out=mcol[:, jj:jj + 1], in_=bank(jj % 2)[:, 0:wcols],
                            axis=AX.X, op=ALU.max,
                            negate=True).then_inc("vv", 1)
                        ctx.wait(ctx.V, "vv")
                        ctx.V.tensor_scalar(out=mcol[:, jj:jj + 1],
                                            in0=mcol[:, jj:jj + 1],
                                            scalar1=100.0, scalar2=-100.0,
                                            op0=ALU.min,
                                            op1=ALU.max).then_inc("vv", 1)
                        ctx.wait(ctx.TE, "vv")
                        ctx.wait(ctx.TE, "pe")
                        ctx.wait(ctx.TE, "ss")
                        w_ = ctx.TE.transpose(out=bank(2 + jj % 2)[0:1, 0:P],
                                              in_=mcol[:, jj:jj + 1],
                                              identity=ident_f[:])
                        w_.then_inc("pe", 1)
                        ctx.wait(ctx.S, "pe")
                        ctx.S.activation(
                            out=qbuf[64:65, qoff + hh * 512 + qsub * P:
                                     qoff + hh * 512 + (qsub + 1) * P],
                            in_=bank(2 + jj % 2)[0:1, 0:P], func=AF.Copy,
                            bias=0.0, scale=1.0).then_inc("ss", 1)
                    if j < 16:
                        hh, qsub = divmod(j, 4)
                        wcols = (qsub + 1) * P
                        if j < 2:
                            ctx.wait(ctx.TE, "dw2")
                            ctx.wait(ctx.TE, "ss")
                            ctx.wait(ctx.TE, "vv", back=5 if after_epi else 0)
                        else:
                            ctx.wait(ctx.TE, "vv")
                        w_ = ctx.TE.matmul(
                            out=bank(j % 2)[:, 0:wcols],
                            lhsT=qbuf[0:HD, qoff + hh * 512 + qsub * P:
                                      qoff + hh * 512 + (qsub + 1) * P],
                            rhs=KbD[0:HD, hh * 512:hh * 512 + wcols],
                            start=True, stop=True)
                        w_.then_inc("pe", 1)

            def emit_scores_exp(ctx, p, ktl, kbv, diag):
                qoff = (p % 2) * 2048
                ep = ktl % 2
                lo = ktl * P if diag else 0
                for pr in range(2):
                    h0 = 2 * pr
                    ctx.wait(ctx.TE, "ss", back=1)
                    for hh in (h0, h0 + 1):
                        w_ = ctx.TE.matmul(
                            out=bank(hh),
                            lhsT=kbv[:, ds(hh * 512 + ktl * P, P)],
                            rhs=qbuf[:, qoff + hh * 512:qoff + (hh + 1) * 512],
                            start=True, stop=True)
                    w_.then_inc("pe", 1)
                    if diag:
                        ctx.wait(ctx.V, "pe")
                        for hh in (h0, h0 + 1):
                            ctx.V.tensor_tensor(
                                out=bank(hh)[:, ktl * P:(ktl + 1) * P],
                                in0=bank(hh)[:, ktl * P:(ktl + 1) * P],
                                in1=trimaskT[:], op=ALU.add).then_inc("vv", 1)
                        ctx.wait(ctx.S, "vv")
                    else:
                        ctx.wait(ctx.S, "pe")
                    expin = psALL[:, h0 * 512:(h0 + 2) * 512].rearrange(
                        "p (h q) -> p h q", q=512)[:, :, lo:512]
                    expout = eT2[:, ds(ep * 2048 + h0 * 512, 1024)].rearrange(
                        "p (h q) -> p h q", q=512)[:, :, lo:512]
                    ctx.S.activation(out=expout, in_=expin, func=AF.Exp,
                                     bias=0.0, scale=1.0).then_inc("ss", 1)

            def emit_av(ctx, p, ktl, vbv, diag, stop, ss_back):
                ep = ktl % 2
                lo = ktl * P if diag else 0
                for pr in range(2):
                    h0 = 2 * pr
                    ctx.wait(ctx.TE, "ss", back=ss_back - pr)
                    for hh in (h0, h0 + 1):
                        w_ = ctx.TE.matmul(
                            out=bank(4 + hh)[0:HD + 1, lo:512],
                            lhsT=vbv[:, ds((ktl * 4 + hh) * (HD + 1), HD + 1)],
                            rhs=eT2[:, ds(ep * 2048 + hh * 512 + lo, 512 - lo)],
                            start=False, stop=stop)
                    w_.then_inc("pe", 1)

            def emit_chunk(ctx, p, kbv, vbv, diag, dkv_back):
                for ktl in range(4):
                    if ktl == 0:
                        ctx.wait(ctx.TE, "dkv", back=dkv_back)
                    emit_scores_exp(ctx, p, ktl, kbv, diag)
                    if ktl > 0:
                        emit_av(ctx, p, ktl - 1, vbv, diag, stop=False,
                                ss_back=3)
                emit_av(ctx, p, 3, vbv, diag, stop=diag, ss_back=1)

            def load_chunk(ctx2, p, cidx, par, inc=True):
                ctx2.SY.dma_start(
                    out=Kb[0:HD, par * 2048:(par + 1) * 2048].rearrange(
                        "d (h t) -> d h t", t=512),
                    in_=KsV[:, 4 * p:4 * p + 4, ds(cidx * 512, 512)]
                    ).then_inc("dkv", 16)
                ctx2.SY.dma_start(
                    out=Vb[:, par * 1040:(par + 1) * 1040].rearrange(
                        "p (kt h c) -> p kt h c", h=4, c=HD + 1),
                    in_=VsV[:, ds(cidx * 4, 4), :].rearrange(
                        "p kt (h c) -> p kt h c", c=HD + 1)
                    [:, :, 4 * p:4 * p + 4, :]).then_inc("dkv", 16)

            def make_pair_body(p):
                def pair_body(ctx2, j):
                    emit_chunk(ctx2, p, Kb[:, 0:2048], Vb[:, 0:1040],
                               diag=False, dkv_back=32)
                    ctx2.wait(ctx2.SY, "pe")
                    ctx2.wait(ctx2.SY, "dkv")
                    load_chunk(ctx2, p, (2 * j + 2) % NG, 0)
                    emit_chunk(ctx2, p, Kb[:, 2048:4096], Vb[:, 1040:2080],
                               diag=False, dkv_back=32)
                    ctx2.wait(ctx2.SY, "pe")
                    ctx2.wait(ctx2.SY, "dkv")
                    load_chunk(ctx2, p, (2 * j + 3) % NG, 1)
                return pair_body

            def make_tail_body(p):
                def tail_body(ctx2, t):
                    emit_chunk(ctx2, p, Kb[:, 0:2048], Vb[:, 0:1040],
                               diag=False, dkv_back=0)
                return tail_body

            def core_body(ctx, qt):
                ctx.wait(ctx.TE, "gg")
                ctx.wait(ctx.TE, "dw2")
                for p in range(4):
                    if p > 0:
                        emit_epi_head(ctx, p - 1)
                    ctx.wait(ctx.SY, "pe")
                    ctx.wait(ctx.SY, "dkv")
                    load_chunk(ctx, p, 0, 0)
                    ctx.SY.dma_start(
                        out=VbDg[:].rearrange(
                            "p (kt h c) -> p kt h c", h=4, c=HD + 1),
                        in_=VsV[:, ds(qt * 4, 4), :].rearrange(
                            "p kt (h c) -> p kt h c", c=HD + 1)
                        [:, :, 4 * p:4 * p + 4, :]).then_inc("dkv", 16)
                    ctx.wait(ctx.SY, "dkv")
                    load_chunk(ctx, p, 1 % NG, 1)
                    emit_diag_prep(ctx, p, qt, after_epi=(p > 0))
                    pn = (p + 1) % 4
                    qq = qt if p < 3 else (qt + 1) % NG
                    ctx.wait(ctx.G, "dw2")
                    ctx.wait(ctx.G, "pe")
                    ctx.G.dma_start(
                        out=qbuf[0:HD, ds((pn % 2) * 2048, 2048)].rearrange(
                            "d (h t) -> d h t", t=512),
                        in_=QsV[:, 4 * pn:4 * pn + 4, ds(qq * 512, 512)]
                        ).then_inc("dw2", 16)
                    if p > 0:
                        emit_epi_tail(ctx, p - 1)
                    ctx.wait(ctx.TE, "vv")
                    ctx.wait(ctx.TE, "ss")
                    ctx.wait(ctx.TE, "dq")
                    ctx.wait(ctx.TE, "pe")
                    w_ = None
                    for hh in range(4):
                        w_ = ctx.TE.matmul(
                            out=psALL[0:HD + 1, (4 + hh) * 512:(5 + hh) * 512],
                            lhsT=zero_row[:, 0:HD + 1],
                            rhs=ones_row[:], start=True, stop=False)
                    w_.then_inc("pe", 1)
                    ctx.inner(qt // 2, make_pair_body(p))
                    ctx.inner(qt % 2, make_tail_body(p))
                    emit_chunk(ctx, p, KbD, VbDg, diag=True, dkv_back=0)
                    ctx.wait(ctx.G, "dw2")
                    ctx.wait(ctx.G, "pe")
                    ctx.G.dma_start(
                        out=KbD[0:HD, :].rearrange("d (h t) -> d h t", t=512),
                        in_=KsV[:, 4 * pn:4 * pn + 4, ds(qq * 512, 512)]
                        ).then_inc("dw2", 16)
                emit_epi_head(ctx, 3)
                emit_epi_tail(ctx, 3)

                for qsub in range(4):
                    for oh in range(2):
                        gi = qsub * 2 + oh
                        bk = gi % 2 + 2
                        ctx.wait(ctx.TE, "st")
                        ctx.wait(ctx.TE, "vv", back=1 if gi > 0 else 0)
                        ctx.wait(ctx.TE, "ss")
                        for dc in range(8):
                            w_ = ctx.TE.matmul(
                                out=bank(bk),
                                lhsT=Oqt[:, dc * 512 + qsub * P:dc * 512 + (qsub + 1) * P],
                                rhs=wpTs[:].rearrange("p (dc o) -> p dc o", o=D)
                                [:, dc, oh * 512:(oh + 1) * 512],
                                start=(dc == 0), stop=(dc == 7))
                        w_.then_inc("pe", 1)
                        ctx.wait(ctx.V, "pe")
                        ctx.V.scalar_tensor_tensor(
                            out=stream[:, ds(qt * (4 * D) + qsub * D + oh * 512, 512)],
                            in0=bank(bk), scalar=1.0,
                            in1=stream[:, ds(qt * (4 * D) + qsub * D + oh * 512, 512)],
                            op0=ALU.mult, op1=ALU.add).then_inc("vv", 1)
                ctx.wait(ctx.TE, "vv")

            def cum_pairs(gv):
                return (gv * gv - 2 * gv + (gv % 2)) // 4
            def cum_tail(gv):
                return gv // 2
            icums = [(cum_pairs, sum(q // 2 for q in range(NG))),
                     (cum_tail, sum(q % 2 for q in range(NG)))] * 4
            SS.run_loop(nc, 0, NG, core_body, inner_cums=icums)
            arena_fence()
            core_ar.free()

        if do_ffn2:
            emit_ffn_phase()

        cf = SS.straight(nc)
        cf.wait(cf.SY, "vv")
        cf.wait(cf.SY, "pe")
        cf.SY.dma_start(out=out.rearrange("(t p) d -> p t d", p=P),
                        in_=stream[:, 0:NT * D].rearrange("p (t d) -> p t d", d=D)
                        ).then_inc("st", 16)
        cf.wait(cf.SY, "st")
        SS.flush_straight()

    return nc


def host_prepare(L0_attn_x, L0_attn_w_qkv, L0_attn_w_proj,
                 L0_ffn_w_gate, L0_ffn_w_up, L0_ffn_w_down):
    wqkvT = np.ascontiguousarray(np.asarray(L0_attn_w_qkv, dtype=np.float32).T)
    wpT = np.ascontiguousarray(np.asarray(L0_attn_w_proj, dtype=np.float32).T)
    wg = np.ascontiguousarray(np.asarray(L0_ffn_w_gate, dtype=np.float32))
    wu = np.ascontiguousarray(np.asarray(L0_ffn_w_up, dtype=np.float32))
    wd = np.ascontiguousarray(np.asarray(L0_ffn_w_down, dtype=np.float32))
    x = np.asarray(L0_attn_x, dtype=np.float32)
    return x, wqkvT, wpT, wg, wu, wd


def kernel(L0_attn_x, L0_attn_w_qkv, L0_attn_w_proj,
           L0_ffn_w_gate, L0_ffn_w_up, L0_ffn_w_down):
    x, wqkvT, wpT, wg, wu, wd = host_prepare(
        L0_attn_x, L0_attn_w_qkv, L0_attn_w_proj,
        L0_ffn_w_gate, L0_ffn_w_up, L0_ffn_w_down)
    B, T, _ = x.shape
    nc = build_nc(T=T, do_ffn1=True, do_attn=True, do_ffn2=True)
    nc.compile()
    in_maps = [{"x": np.ascontiguousarray(x[b]), "wg": wg, "wu": wu, "wd": wd,
                "wqkvT": wqkvT, "wpT": wpT} for b in range(B)]
    res = run_bass_kernel_spmd(nc, in_maps, core_ids=list(range(8)))
    return np.stack([np.asarray(r["out"]) for r in res.results], axis=0)
